# version 66
# baseline (speedup 1.0000x reference)
import contextlib
import numpy as np
import concourse.bass as bass
import concourse.mybir as mybir
from concourse.bass_utils import run_bass_kernel_spmd

F32 = mybir.dt.float32
I32 = mybir.dt.int32
BF16 = mybir.dt.bfloat16
AF = mybir.ActivationFunctionType
ALU = mybir.AluOpType
AX = mybir.AxisListType
LN2C = float(np.log(2.0))
TINY = float(2.0 ** -126)

ENGS = ("tensor", "vector", "scalar", "gpsimd", "sync")

D = 1024
NH = 16
HD = 64
LAT = 512
HID = 4096
QB = 512


class Buf:
    __slots__ = ("w", "rs")

    def __init__(self):
        self.w = None
        self.rs = []


class Prog:

    def __init__(self):
        self.items = {e: [] for e in ENGS}
        self.cnt = {e: 0 for e in ENGS}
        self.dma_cnt = {}

    @staticmethod
    def _collect(reads, wbufs, extra):
        deps = []
        for b in reads:
            if b.w is not None:
                deps.append(b.w)
        for b in wbufs:
            if b.w is not None:
                deps.append(b.w)
            deps.extend(b.rs)
        deps.extend(e for e in extra if e is not None)
        return deps

    def op(self, eng, fn, reads=(), writes=(), extra=(), wfirst=None, wlast=True):
        wbufs = list(writes) if wfirst is None else list(wfirst)
        deps = self._collect(reads, wbufs, extra) + self.take_pending(eng)
        self.items[eng].append(("op", fn, tuple(deps)))
        self.cnt[eng] += 1
        ev = (eng, self.cnt[eng])
        for b in reads:
            b.rs.append(ev)
        if wlast:
            for b in writes:
                b.w = ev
                b.rs = []
        return ev

    def dma(self, qeng, sem, out_ap, in_ap, reads=(), writes=(), extra=()):
        deps = self._collect(reads, writes, extra) + self.take_pending(qeng)
        self.dma_cnt[sem] = self.dma_cnt.get(sem, 0) + 16
        val = self.dma_cnt[sem]
        self.items[qeng].append(("dma", (out_ap, in_ap, sem), tuple(deps)))
        ev = (sem, val)
        for b in reads:
            b.rs.append(ev)
        for b in writes:
            b.w = ev
            b.rs = []
        return ev

    def barrier(self):
        evs = [(e, self.cnt[e]) for e in ENGS]
        evs += [(s, v) for s, v in self.dma_cnt.items()]
        self.pending = {e: [ev for ev in evs if ev[0] != e and ev[1] > 0] for e in ENGS}

    def take_pending(self, eng):
        p = getattr(self, "pending", None)
        if p and eng in p:
            return p.pop(eng)
        return []

    def mark(self):
        return {e: len(self.items[e]) for e in ENGS}


def _replay(nc, P, sems, lo, hi, waited_all):
    with nc.Block() as blk:
        def mk(eng):
            def body(handle):
                waited = waited_all[eng]
                for kind, a, deps in P.items[eng][lo[eng]:hi[eng]]:
                    for (sn, val) in deps:
                        if waited.get(sn, 0) >= val:
                            continue
                        handle.wait_ge(sems[sn], val)
                        waited[sn] = val
                    if kind == "op":
                        a(handle).then_inc(sems[eng], 1)
                    else:
                        out_ap, in_ap, sn = a
                        handle.dma_start(out=out_ap, in_=in_ap).then_inc(sems[sn], 16)
            return body
        blk.tensor(mk("tensor"))
        blk.vector(mk("vector"))
        blk.scalar(mk("scalar"))
        blk.gpsimd(mk("gpsimd"))
        blk.sync(mk("sync"))


def build_program(T):
    NT = T // 128
    NQB = T // QB
    NG = NT // 4
    NST = NT // 2

    nc = bass.Bass("TRN2", target_bir_lowering=False)
    P = Prog()

    x_h = nc.dram_tensor("x", [T, D], F32, kind="ExternalInput").ap()
    wdn_h = nc.dram_tensor("wdn", [D, LAT], F32, kind="ExternalInput").ap()
    wqu_h = nc.dram_tensor("wqu", [LAT, D], F32, kind="ExternalInput").ap()
    wkv_h = nc.dram_tensor("wkv", [LAT, 2 * HD], F32, kind="ExternalInput").ap()
    wcp_h = nc.dram_tensor("wcp", [D, D], F32, kind="ExternalInput").ap()
    wfc_h = nc.dram_tensor("wfc", [HID, D], F32, kind="ExternalInput").ap()
    bfc_h = nc.dram_tensor("bfc", [1, HID], F32, kind="ExternalInput").ap()
    wpj_h = nc.dram_tensor("wpj", [D, HID], F32, kind="ExternalInput").ap()
    bpj_h = nc.dram_tensor("bpj", [1, D], F32, kind="ExternalInput").ap()
    ident_h = nc.dram_tensor("ident", [128, 128], F32, kind="ExternalInput").ap()
    tri_h = nc.dram_tensor("tri", [128, 128], F32, kind="ExternalInput").ap()
    sel_h = nc.dram_tensor("sel", [128, 256], F32, kind="ExternalInput").ap()
    x2_h = nc.dram_tensor("x2s", [T, D], F32, kind="Internal").ap()
    wfcT_h = nc.dram_tensor("wfcTs", [8, 128, HID], BF16, kind="Internal").ap()
    lat_h = nc.dram_tensor("lats", [4, 128, T], BF16, kind="Internal").ap()
    out_h = nc.dram_tensor("out", [T, D], F32, kind="ExternalOutput").ap()

    x2_bufs = [Buf() for _ in range(NT)]
    wfcT_bufs = [Buf() for _ in range(32)]

    dma_sem_names = [
        "xa0", "xa1", "xb0", "xb1", "x2o0", "x2o1", "x2i0", "x2i1", "x2i2", "x2i3",
        "oo0", "oo1", "wsp0", "wsp1", "wld0", "wld1", "wld2", "kf2s",
        "gwin", "wfq0", "wfq1", "wpq0", "wpq1", "gbias", "lsp", "lld0", "lld1", "wre0", "wre1",
    ]

    with contextlib.ExitStack() as stack:
        ec = stack.enter_context
        ident_sb = ec(nc.sbuf_tensor("ident_sb", [128, 128], BF16))
        tri_sb = ec(nc.sbuf_tensor("tri_sb", [128, 128], BF16))
        ones_sb = ec(nc.sbuf_tensor("ones_sb", [1, 128], BF16))
        wpjT = ec(nc.sbuf_tensor("wpjT", [128, 32, D], BF16))
        wqt_ps = ec(nc.psum_tensor("wqt_ps", [128, 1024], BF16))
        sems = {e: ec(nc.semaphore("s_" + e)) for e in ENGS}
        for n_ in dma_sem_names:
            sems[n_] = ec(nc.semaphore(n_))

        waited_all = {e: {} for e in ENGS}
        b_ident = Buf(); b_tri = Buf(); b_ones = Buf()
        b_wpjT = Buf(); b_wqt = Buf()

        with contextlib.ExitStack() as ab_stack:
            latg = ab_stack.enter_context(nc.sbuf_tensor("latg", [128, 4, QB], BF16))
            latB = ab_stack.enter_context(nc.sbuf_tensor("latB", [128, 1, 4, QB], BF16))
            kf = ab_stack.enter_context(nc.sbuf_tensor("kf", [128, T], BF16))
            v_tm = ab_stack.enter_context(nc.sbuf_tensor("v_tm", [128, NT, 66], BF16))
            wdn_sb = ab_stack.enter_context(nc.sbuf_tensor("wdn_sb", [128, 8, LAT], BF16))
            wqu_sb = ab_stack.enter_context(nc.sbuf_tensor("wqu_sb", [128, 4, D], BF16))
            wkv_sb = ab_stack.enter_context(nc.sbuf_tensor("wkv_sb", [128, 4, 2 * HD], BF16))
            wcp_sb = ab_stack.enter_context(nc.sbuf_tensor("wcp_sb", [128, 8, D], BF16))
            xA = ab_stack.enter_context(nc.sbuf_tensor("xA", [128, 2, D], F32))
            ln1_sb = ab_stack.enter_context(nc.sbuf_tensor("ln1_sb", [128, 2, D], BF16))
            ln1T = ab_stack.enter_context(nc.sbuf_tensor("ln1T", [128, 8, QB], BF16))
            kv_sb = ab_stack.enter_context(nc.sbuf_tensor("kv_sb", [128, QB], BF16))
            q_sb = ab_stack.enter_context(nc.sbuf_tensor("q_sb", [128, 8, QB], BF16))
            pT = ab_stack.enter_context(nc.sbuf_tensor("pT", [128, 2, 2, 2 * QB], BF16))
            sel_sb = ab_stack.enter_context(nc.sbuf_tensor("sel_sb", [128, 256], BF16))
            dens_sb = ab_stack.enter_context(nc.sbuf_tensor("dens", [128, QB], F32))
            recba_sb = ab_stack.enter_context(nc.sbuf_tensor("recba", [128, QB], BF16))
            y_sb = ab_stack.enter_context(nc.sbuf_tensor("y_sb", [128, 8, QB], BF16))
            bcs_sb = ab_stack.enter_context(nc.sbuf_tensor("bcs", [128, QB], BF16))
            lnsm = ab_stack.enter_context(nc.sbuf_tensor("lnsm", [128, 2, 8], F32))
            wqf = ab_stack.enter_context(nc.sbuf_tensor("wqf", [128, 2, D], F32))
            wqb = ab_stack.enter_context(nc.sbuf_tensor("wqb", [128, D], BF16))
            wstg = ab_stack.enter_context(nc.sbuf_tensor("wstg", [128, 8, 128], BF16))
            wpf = ab_stack.enter_context(nc.sbuf_tensor("wpf", [128, 2, HID // 2], F32))
            wpb = ab_stack.enter_context(nc.sbuf_tensor("wpb", [128, HID // 2], BF16))
            qs1 = ab_stack.enter_context(nc.sbuf_tensor("qs1", [128, 2, 4, 32], F32))
            qs2 = ab_stack.enter_context(nc.sbuf_tensor("qs2", [128, 2, 4, 64], F32))
            zq = ab_stack.enter_context(nc.sbuf_tensor("zq", [128, 512], F32))
            mq = ab_stack.enter_context(nc.sbuf_tensor("mq", [128, 512], I32))
            qi1 = ab_stack.enter_context(nc.sbuf_tensor("qi1", [128, 64], I32))
            bc_ps = ab_stack.enter_context(nc.psum_tensor("bc_ps", [64, 512], F32))
            se_ps = ab_stack.enter_context(nc.psum_tensor("se_ps", [128, 1024], F32))
            so_ps = ab_stack.enter_context(nc.psum_tensor("so_ps", [128, 1024], F32))
            y0 = ab_stack.enter_context(nc.psum_tensor("y0", [128, 512], F32))
            y1 = ab_stack.enter_context(nc.psum_tensor("y1", [128, 512], F32))
            _dummy_ab = 0
            b_latg = Buf()
            b_latB = [Buf()]
            lat_bufs = [Buf() for _ in range(64)]
            b_kf = Buf(); b_kf2 = Buf()
            b_vtm = [Buf() for _ in range(NT)]
            b_wdn = Buf(); b_wqu = Buf(); b_wkv = Buf(); b_wcp = Buf()
            b_xA = [Buf(), Buf()]
            b_ln1 = [Buf(), Buf()]
            b_ln1T = Buf()
            b_kvsb = Buf()
            b_q = [Buf() for _ in range(8)]
            b_pT = [[Buf(), Buf()], [Buf(), Buf()]]
            b_sel = Buf(); b_dens = Buf(); b_recba = Buf()
            b_y = [Buf() for _ in range(8)]
            b_bcs = Buf()
            b_lnsm = [Buf(), Buf()]
            b_wqf = [Buf(), Buf()]
            b_wqb = Buf(); b_wstg = Buf()
            b_wpf = [Buf(), Buf()]
            b_wpb = Buf()
            b_qs1 = [Buf(), Buf()]; b_qs2 = [Buf(), Buf()]
            b_zq = Buf(); b_mq = Buf(); b_qi1 = Buf()
            b_bc = Buf()
            b_s = [[Buf(), Buf()], [Buf(), Buf()]]
            b_se = b_s[0][0]; b_so = b_s[1][0]
            b_y_ps = [Buf(), Buf()]

            P.dma("gpsimd", "gwin", ident_sb[:], ident_h, writes=[b_ident])
            P.dma("gpsimd", "gwin", tri_sb[:], tri_h, writes=[b_tri])
            P.dma("gpsimd", "gwin", sel_sb[:], sel_h, writes=[b_sel])
            P.dma("gpsimd", "gwin", wdn_sb[:], wdn_h.rearrange("(c p) m -> p c m", p=128), writes=[b_wdn])
            P.dma("gpsimd", "gwin", wqu_sb[:], wqu_h.rearrange("(c p) m -> p c m", p=128), writes=[b_wqu])
            P.dma("gpsimd", "gwin", wkv_sb[:], wkv_h.rearrange("(c p) m -> p c m", p=128), writes=[b_wkv])
            gw_total = P.dma("gpsimd", "gwin", wcp_sb[:], wcp_h.rearrange("(c p) m -> p c m", p=128), writes=[b_wcp])
            for b in (b_ident, b_tri, b_sel, b_wdn, b_wqu, b_wkv, b_wcp):
                b.w = gw_total

            P.op("vector", lambda h: h.memset(ones_sb[:], 1.0), writes=[b_ones])
            P.op("vector", lambda h: h.memset(dens_sb[:], 1.0), writes=[b_dens])
            ev_vones = P.op("vector", lambda h: h.memset(v_tm[:, :, 64:65], 1.0))
            for b in b_vtm:
                b.w = ev_vones

            def layernorm_thunks(xbuf, x_ap, out_ap, out_buf, sm, smb, ztmp, zbuf, ztmp_w=None,
                                 sq_gpsimd=False):
                if ztmp_w is None:
                    sq = [
                        (lambda: P.op("gpsimd", lambda h: h.tensor_tensor(ztmp, x_ap, x_ap, ALU.mult),
                                      reads=[xbuf], writes=[zbuf])) if sq_gpsimd else
                        (lambda: P.op("scalar", lambda h: h.activation(ztmp, x_ap, AF.Square),
                                      reads=[xbuf], writes=[zbuf])),
                        lambda: P.op("vector", lambda h: h.reduce_sum(sm[:, 1:2], ztmp, AX.X),
                                     reads=[zbuf], writes=[smb]),
                    ]
                else:
                    sq = [
                        lambda: P.op("scalar", lambda h: h.activation(ztmp, x_ap[:, 0:ztmp_w], AF.Square),
                                     reads=[xbuf], writes=[zbuf]),
                        lambda: P.op("vector", lambda h: h.reduce_sum(sm[:, 1:2], ztmp, AX.X),
                                     reads=[zbuf], writes=[smb]),
                        lambda: P.op("scalar", lambda h: h.activation(ztmp, x_ap[:, ztmp_w:D], AF.Square),
                                     reads=[xbuf], writes=[zbuf]),
                        lambda: P.op("vector", lambda h: h.reduce_sum(sm[:, 3:4], ztmp, AX.X),
                                     reads=[zbuf], writes=[smb]),
                        lambda: P.op("vector", lambda h: h.tensor_tensor(sm[:, 1:2], sm[:, 1:2], sm[:, 3:4], ALU.add),
                                     reads=[smb], writes=[smb]),
                    ]
                return [
                    lambda: P.op("vector", lambda h: h.reduce_sum(sm[:, 0:1], x_ap, AX.X),
                                 reads=[xbuf], writes=[smb]),
                ] + sq + [
                    lambda: P.op("vector", lambda h: h.tensor_scalar_mul(sm[:, 2:3], sm[:, 0:1], 1.0 / D),
                                 reads=[smb], writes=[smb]),
                    lambda: P.op("vector", lambda h: h.tensor_tensor(sm[:, 3:4], sm[:, 2:3], sm[:, 2:3], ALU.mult),
                                 reads=[smb], writes=[smb]),
                    lambda: P.op("vector", lambda h: h.scalar_tensor_tensor(sm[:, 4:5], sm[:, 1:2], 1.0 / D, sm[:, 3:4], ALU.mult, ALU.subtract),
                                 reads=[smb], writes=[smb]),
                    lambda: P.op("vector", lambda h: h.tensor_scalar_add(sm[:, 4:5], sm[:, 4:5], 1e-5),
                                 reads=[smb], writes=[smb]),
                    lambda: P.op("scalar", lambda h: h.activation(sm[:, 5:6], sm[:, 4:5], AF.Ln),
                                 reads=[smb], writes=[smb]),
                    lambda: P.op("scalar", lambda h: h.activation(sm[:, 6:7], sm[:, 5:6], AF.Exp, scale=-0.5),
                                 reads=[smb], writes=[smb]),
                    lambda: P.op("vector", lambda h: h.scalar_tensor_tensor(sm[:, 7:8], sm[:, 2:3], -1.0, sm[:, 6:7], ALU.mult, ALU.mult),
                                 reads=[smb], writes=[smb]),
                    lambda: P.op("scalar", lambda h: h.activation(out_ap, x_ap, AF.Identity, bias=sm[:, 7:8], scale=sm[:, 6:7]),
                                 reads=[xbuf, smb], writes=[out_buf]),
                ]

            def layernorm(xbuf, x_ap, out_ap, out_buf, sm, smb, ztmp, zbuf, ztmp_w=None):
                for th in layernorm_thunks(xbuf, x_ap, out_ap, out_buf, sm, smb, ztmp, zbuf, ztmp_w):
                    th()

            def interleave(*lists):
                n = max(len(l) for l in lists)
                for i in range(n):
                    for l in lists:
                        if i < len(l):
                            l[i]()

            def a_ln_pair(p):
                chains = []
                for sl in range(2):
                    ti = 2 * p + sl
                    P.dma("sync", "xa%d" % sl, xA[:, sl, :], x_h[ti * 128:(ti + 1) * 128, :],
                          writes=[b_xA[sl]])
                    ztmp = (zq[:], dens_sb[:])[sl]
                    zbuf = (b_zq, b_dens)[sl]
                    chains.append(layernorm_thunks(
                        b_xA[sl], xA[:, sl, :], ln1_sb[:, sl, :], b_ln1[sl],
                        lnsm[:, sl, :], b_lnsm[sl], ztmp, zbuf, ztmp_w=512))
                interleave(*chains)

            def a_trans_pair(p):
                for sl in range(2):
                    i = 2 * (p % 2) + sl
                    for j in range(8):
                        P.op("tensor",
                             (lambda sl=sl, j=j: (lambda h: h.matmul(
                                 wqt_ps[:, j * 128:(j + 1) * 128], ln1_sb[:, sl, j * 128:(j + 1) * 128],
                                 ident_sb[:], start=(j == 0), stop=(j == 7), is_transpose=True)))(),
                             reads=[b_ln1[sl], b_ident],
                             wfirst=[b_wqt] if j == 0 else [], writes=[b_wqt], wlast=(j == 7))
                    P.op("vector",
                         (lambda i=i: (lambda h: h.tensor_copy(
                             ln1T[:, :, i * 128:(i + 1) * 128],
                             wqt_ps[:].rearrange("p (c q) -> p c q", q=128))))(),
                         reads=[b_wqt], writes=[b_ln1T])

            def a_latent_kv(g):
                for lc in range(4):
                    ps = (se_ps[:, 0:512], se_ps[:, 512:1024], so_ps[:, 0:512], so_ps[:, 512:1024])[lc]
                    pb = (b_s[0][0], b_s[0][1], b_s[1][0], b_s[1][1])[lc]
                    for f in range(8):
                        P.op("tensor",
                             (lambda lc=lc, f=f, ps=ps: (lambda h: h.matmul(
                                 ps, wdn_sb[:, f, lc * 128:(lc + 1) * 128], ln1T[:, f, :],
                                 start=(f == 0), stop=(f == 7))))(),
                             reads=[b_wdn, b_ln1T],
                             wfirst=[pb] if f == 0 else [], writes=[pb], wlast=(f == 7))
                    P.op("vector",
                         (lambda lc=lc, ps=ps: (lambda h: h.tensor_copy(
                             latg[:, lc, :], ps)))(),
                         reads=[pb], writes=[b_latg])
                for l in range(4):
                    P.op("tensor",
                         (lambda l=l: (lambda h: h.matmul(
                             se_ps[:, 0:512], wkv_sb[:, l, :], latg[:, l, :],
                             start=(l == 0), stop=(l == 3))))(),
                         reads=[b_wkv, b_latg],
                         wfirst=[b_se] if l == 0 else [], writes=[b_se], wlast=(l == 3))
                P.dma("sync", "lsp", lat_h[:, :, g * QB:(g + 1) * QB].rearrange("c p q -> p c q"),
                      latg[:], reads=[b_latg], writes=[lat_bufs[g]])
                P.op("vector", lambda h: h.tensor_copy(kv_sb[:], se_ps[:, 0:512]),
                     reads=[b_se], writes=[b_kvsb])
                P.op("vector",
                     (lambda g=g: (lambda h: h.tensor_copy(kf[0:64, g * QB:(g + 1) * QB], kv_sb[0:64, :])))(),
                     reads=[b_kvsb], writes=[b_kf])
                for j in range(4):
                    P.op("tensor",
                         (lambda j=j: (lambda h: h.matmul(
                             wqt_ps[:, j * 128:(j + 1) * 128], kv_sb[:, j * 128:(j + 1) * 128],
                             ident_sb[:], start=(j == 0), stop=(j == 3), is_transpose=True)))(),
                         reads=[b_kvsb, b_ident],
                         wfirst=[b_wqt] if j == 0 else [], writes=[b_wqt], wlast=(j == 3))
                for j in range(4):
                    ti = 4 * g + j
                    P.op("vector",
                         (lambda j=j, ti=ti: (lambda h: h.tensor_copy(
                             v_tm[:, ti, 0:64], wqt_ps[:, j * 128 + 64:(j + 1) * 128])))(),
                         reads=[b_wqt], writes=[b_vtm[ti]])

            a_ln_pair(0)
            for p in range(2 * NG):
                a_trans_pair(p)
                if p + 1 < 2 * NG:
                    a_ln_pair(p + 1)
                if p % 2 == 1:
                    a_latent_kv(p // 2)
            P.dma("sync", "kf2s", kf[64:128, :], kf[0:64, :], reads=[b_kf], writes=[b_kf2])
            P.op("vector", lambda h: h.memset(dens_sb[:], 1.0), writes=[b_dens])

            def quant_amax(src_buf, src_ap, sm_buf, sm_ap, G):
                src3 = src_ap.rearrange("p (g e) -> p g e", e=32)
                am = sm_ap[:, 0, :]
                P.op("vector", (lambda am=am, src3=src3: (lambda h: h.tensor_reduce(
                        am.rearrange("p (g o) -> p g o", o=1), src3, AX.X, ALU.max,
                        apply_absolute_value=True)))(),
                     reads=[src_buf], writes=[sm_buf])
                P.op("gpsimd", (lambda am=am: (lambda h: h.tensor_scalar_max(am, am, TINY)))(),
                     reads=[sm_buf], writes=[sm_buf])

            def scale_bits_thunks(sm_buf, sm_ap):
                am_i = sm_ap[:, 0, :].bitcast(I32)
                sc = sm_ap[:, 2, :]; rs = sm_ap[:, 3, :]
                sc_i = sc.bitcast(I32)
                return [
                    (lambda sc_i=sc_i, am_i=am_i, sm_buf=sm_buf: (lambda: P.op(
                        "vector", lambda h: h.tensor_scalar(
                            sc_i, am_i, 0x7F800000, None, ALU.bitwise_and),
                        reads=[sm_buf], writes=[sm_buf])))(),
                    (lambda sc=sc, sm_buf=sm_buf: (lambda: P.op(
                        "vector", lambda h: h.tensor_scalar_mul(sc, sc, 0.125),
                        reads=[sm_buf], writes=[sm_buf])))(),
                    (lambda rs=rs, sc=sc, sm_buf=sm_buf: (lambda: P.op(
                        "vector", lambda h: h.reciprocal(rs, sc),
                        reads=[sm_buf], writes=[sm_buf])))(),
                ]

            def quant_scales(sm_buf, sm_ap, G):
                for th in scale_bits_thunks(sm_buf, sm_ap):
                    th()

            def quant_rounds(src_buf, src_ap, dst_buf, dst_ap, sm_buf, sm_ap, G):
                W = 32 * G
                for w0 in range(0, W, 512):
                    ww = min(512, W - w0)
                    gg0 = w0 // 32
                    ggn = ww // 32
                    s3 = src_ap[:, w0:w0 + ww].rearrange("p (g e) -> p g e", e=32)
                    d3 = dst_ap[:, w0:w0 + ww].rearrange("p (g e) -> p g e", e=32)
                    rsb = sm_ap[:, 3, gg0:gg0 + ggn].to_broadcast([128, ggn, 32])
                    scb = sm_ap[:, 2, gg0:gg0 + ggn].to_broadcast([128, ggn, 32])
                    P.op("gpsimd", (lambda s3=s3, rsb=rsb, ww=ww: (lambda h: h.tensor_tensor(
                            zq[:, 0:ww].rearrange("p (g e) -> p g e", e=32), s3, rsb, ALU.mult)))(),
                         reads=[src_buf, sm_buf], writes=[b_zq])
                    P.op("vector", (lambda ww=ww: (lambda h: h.tensor_copy(mq[:, 0:ww], zq[:, 0:ww])))(),
                         reads=[b_zq], writes=[b_mq])
                    P.op("vector", (lambda d3=d3, scb=scb, ww=ww: (lambda h: h.tensor_tensor(
                            d3, mq[:, 0:ww].rearrange("p (g e) -> p g e", e=32), scb, ALU.mult)))(),
                         reads=[b_mq, sm_buf], writes=[dst_buf])

            def wfc_s1(i):
                sl = i % 2
                P.dma("gpsimd", "wfq%d" % sl, wqf[:, sl, :], wfc_h[i * 128:(i + 1) * 128, :],
                      writes=[b_wqf[sl]])

            def wfc_s2(i):
                sl = i % 2
                quant_amax(b_wqf[sl], wqf[:, sl, :], b_qs1[sl], qs1[:, sl, :, :], 32)
                quant_scales(b_qs1[sl], qs1[:, sl, :, :], 32)
                quant_rounds(b_wqf[sl], wqf[:, sl, :], b_wqb, wqb[:], b_qs1[sl], qs1[:, sl, :, :], 32)

            def wfc_s3(i):
                for j in range(8):
                    P.op("tensor",
                         (lambda j=j: (lambda h: h.matmul(
                             wqt_ps[:, j * 128:(j + 1) * 128], wqb[:, j * 128:(j + 1) * 128],
                             ident_sb[:], start=(j == 0), stop=(j == 7), is_transpose=True)))(),
                         reads=[b_wqb, b_ident],
                         wfirst=[b_wqt] if j == 0 else [], writes=[b_wqt], wlast=(j == 7))

            def wfc_s4(i):
                sl = i % 2
                P.op("vector", lambda h: h.tensor_copy(
                        wstg[:], wqt_ps[:].rearrange("p (c q) -> p c q", q=128)),
                     reads=[b_wqt], writes=[b_wstg])
                P.dma("sync", "wsp%d" % sl,
                      wfcT_h[:, :, i * 128:(i + 1) * 128].rearrange("f p q -> p f q"),
                      wstg[:], reads=[b_wstg], writes=[wfcT_bufs[i]])

            def wpj_s1(c, half):
                sl = half
                P.dma("gpsimd", "wpq%d" % sl, wpf[:, sl, :],
                      wpj_h[c * 128:(c + 1) * 128, half * 2048:(half + 1) * 2048],
                      writes=[b_wpf[sl]])

            def wpj_s2(c, half):
                sl = half
                quant_amax(b_wpf[sl], wpf[:, sl, :], b_qs2[sl], qs2[:, sl, :, :], 64)
                quant_scales(b_qs2[sl], qs2[:, sl, :, :], 64)
                quant_rounds(b_wpf[sl], wpf[:, sl, :], b_wpb, wpb[:], b_qs2[sl], qs2[:, sl, :, :], 64)

            def wpj_s3(c, half):
                for g2 in range(2):
                    if g2 == 1:
                        grp = half * 2
                        P.op("scalar",
                             (lambda grp=grp, c=c: (lambda h: h.activation(
                                 wpjT[:, grp * 8:(grp + 1) * 8, c * 128:(c + 1) * 128],
                                 wqt_ps[:].rearrange("p (c q) -> p c q", q=128), AF.Identity)))(),
                             reads=[b_wqt], writes=[b_wpjT])
                    for j in range(8):
                        hl = g2 * 8 + j
                        P.op("tensor",
                             (lambda hl=hl, j=j: (lambda h: h.matmul(
                                 wqt_ps[:, j * 128:(j + 1) * 128], wpb[:, hl * 128:(hl + 1) * 128],
                                 ident_sb[:], start=(j == 0), stop=(j == 7), is_transpose=True)))(),
                             reads=[b_wpb, b_ident],
                             wfirst=[b_wqt] if j == 0 else [], writes=[b_wqt], wlast=(j == 7))

            def wpj_s4(c, half):
                grp = half * 2 + 1
                P.op("vector",
                     (lambda grp=grp, c=c: (lambda h: h.tensor_copy(
                         wpjT[:, grp * 8:(grp + 1) * 8, c * 128:(c + 1) * 128],
                         wqt_ps[:].rearrange("p (c q) -> p c q", q=128))))(),
                     reads=[b_wqt], writes=[b_wpjT])

            wq_jobs = []
            for k in range(16):
                wq_jobs.append(("fc", 2 * k))
                wq_jobs.append(("fc", 2 * k + 1))
                if k % 2 == 0:
                    c = k // 2
                    wq_jobs.append(("pj", (c, 0)))
                    wq_jobs.append(("pj", (c, 1)))
            wq_q = [None, None, None, None]

            def wq_front():
                if wq_q[3] is not None:
                    kind, arg = wq_q[3]
                    (wfc_s4 if kind == "fc" else (lambda a: wpj_s4(*a)))(arg)
                    wq_q[3] = None

            def wq_step():
                wq_front()
                if wq_q[2] is not None:
                    kind, arg = wq_q[2]
                    (wfc_s3 if kind == "fc" else (lambda a: wpj_s3(*a)))(arg)
                wq_q[3] = wq_q[2]
                wq_q[2] = wq_q[1]
                if wq_q[1] is not None:
                    kind, arg = wq_q[1]
                    (wfc_s2 if kind == "fc" else (lambda a: wpj_s2(*a)))(arg)
                wq_q[1] = None
                if wq_jobs:
                    kind, arg = wq_jobs.pop(0)
                    (wfc_s1 if kind == "fc" else (lambda a: wpj_s1(*a)))(arg)
                    wq_q[1] = (kind, arg)

            P.dma("sync", "lld0", latB[:, 0, :, :],
                  lat_h[:, :, 0:QB].rearrange("c p q -> p c q"),
                  extra=[lat_bufs[0].w], writes=[b_latB[0]])
            for qb in range(NQB):
                lsl = 0

                qslots = (se_ps[:, 0:512], se_ps[:, 512:1024], so_ps[:, 0:512], so_ps[:, 512:1024])
                qsbufs = (b_s[0][0], b_s[0][1], b_s[1][0], b_s[1][1])
                for oc in range(8):
                    ps = qslots[oc % 4]
                    pb = qsbufs[oc % 4]
                    for l in range(4):
                        P.op("tensor",
                             (lambda oc=oc, l=l, qb=qb, ps=ps, lsl=lsl: (lambda h: h.matmul(
                                 ps, wqu_sb[:, l, oc * 128:(oc + 1) * 128],
                                 latB[:, lsl, l, :],
                                 start=(l == 0), stop=(l == 3))))(),
                             reads=[b_wqu, b_latB[lsl]],
                             wfirst=[pb] if l == 0 else [], writes=[pb], wlast=(l == 3))
                    P.op("scalar",
                         (lambda oc=oc, ps=ps: (lambda h: h.activation(q_sb[:, oc, :], ps, AF.Identity)))(),
                         reads=[pb], writes=[b_q[oc]])

                if qb + 1 < NQB:
                    P.dma("sync", "lld%d" % ((qb + 1) % 2), latB[:, 0, :, :],
                          lat_h[:, :, (qb + 1) * QB:(qb + 2) * QB].rearrange("c p q -> p c q"),
                          extra=[lat_bufs[qb + 1].w], writes=[b_latB[0]])

                nkt = 4 * qb + 4
                npair = nkt // 2

                def emit_qk4(oc, p):
                    for par in range(2):
                        base = 64 * par
                        sps = (se_ps, so_ps)[par]
                        for j in range(2):
                            spb = b_s[par][j]
                            kt = 2 * p + j
                            cc = max(0, 128 * kt - 512 * qb)
                            P.op("tensor",
                                 (lambda sps=sps, base=base, kt=kt, oc=oc, cc=cc, j=j: (lambda h: h.matmul(
                                     sps[:, j * 512 + cc:(j + 1) * 512],
                                     kf[base:base + 64, kt * 128:(kt + 1) * 128],
                                     q_sb[base:base + 64, oc, cc:512], start=True, stop=True)))(),
                                 reads=[b_kf2 if par else b_kf, b_q[oc]],
                                 wfirst=[spb], writes=[spb], wlast=True)

                def emit_exp2(oc, p):
                    cc0 = max(0, 128 * (2 * p) - 512 * qb)
                    sl = p % 2
                    for par in range(2):
                        sps = (se_ps, so_ps)[par]
                        ptb = b_pT[par][sl]
                        P.op("scalar",
                             (lambda par=par, sps=sps, cc0=cc0, sl=sl: (lambda h: h.activation(
                                 pT[:, par, sl, cc0:1024], sps[:, cc0:1024], AF.Exp, scale=0.125)))(),
                             reads=[b_s[par][0], b_s[par][1]], writes=[ptb])
                        for j in range(2):
                            kt = 2 * p + j
                            if kt >= 4 * qb:
                                cc = 128 * kt - 512 * qb
                                col = j * 512 + cc
                                P.op("gpsimd",
                                     (lambda par=par, col=col, sl=sl: (lambda h: h.tensor_tensor(
                                         pT[:, par, sl, col:col + 128], pT[:, par, sl, col:col + 128],
                                         tri_sb[:], ALU.mult)))(),
                                     reads=[b_tri], writes=[ptb])

                def emit_pv4(oc, p):
                    sl = p % 2
                    for par in range(2):
                        yps = (y0, y1)[par]
                        ypb = b_y_ps[par]
                        ptb = b_pT[par][sl]
                        for j in range(2):
                            kt = 2 * p + j
                            cc = max(0, 128 * kt - 512 * qb)
                            P.op("tensor",
                                 (lambda yps=yps, par=par, kt=kt, cc=cc, j=j, nkt=nkt, sl=sl: (lambda h: h.matmul(
                                     yps[0:65, cc:512], v_tm[:, kt, 0:65],
                                     pT[:, par, sl, j * 512 + cc:(j + 1) * 512],
                                     start=(kt == 0), stop=(kt == nkt - 1))))(),
                                 reads=[b_vtm[kt], ptb],
                                 wfirst=[ypb] if kt == 0 else [], writes=[ypb], wlast=(kt == nkt - 1))

                def emit_ycopy(oc):
                    for par in range(2):
                        yps = (y0, y1)[par]
                        ypb = b_y_ps[par]
                        row = 32 * ((2 * oc + par) % 4)
                        P.op("scalar",
                             (lambda yps=yps, row=row: (lambda h: h.activation(
                                 dens_sb[row:row + 1, :], yps[64:65, :], AF.Identity)))(),
                             reads=[ypb], writes=[b_dens])
                        P.op("scalar",
                             (lambda yps=yps, oc=oc, par=par: (lambda h: h.activation(
                                 y_sb[64 * par:64 * par + 64, oc, :], yps[0:64, :], AF.Identity)))(),
                             reads=[ypb], writes=[b_y[oc]])

                def emit_div_recip():
                    def _recip_lp(h):
                        with nc.allow_low_precision(reason="softmax reciprocal consumed as bf16 matmul operand"):
                            return h.reciprocal(recba_sb[:], dens_sb[:])
                    P.op("vector", _recip_lp, reads=[b_dens], writes=[b_recba])

                def emit_div_head(head):
                    oc2, par = head // 2, head % 2
                    m = head % 4
                    P.op("tensor",
                         (lambda m=m: (lambda h: h.matmul(
                             bc_ps[0:64, :], sel_sb[:, m * 64:(m + 1) * 64],
                             recba_sb[:], start=True, stop=True)))(),
                         reads=[b_sel, b_recba], wfirst=[b_bc], writes=[b_bc])
                    P.op("scalar",
                         (lambda par=par: (lambda h: h.activation(
                             bcs_sb[64 * par:64 * par + 64, :], bc_ps[0:64, :], AF.Identity)))(),
                         reads=[b_bc], writes=[b_bcs])
                    P.op("vector",
                         (lambda oc2=oc2, par=par: (lambda h: h.tensor_tensor(
                             y_sb[64 * par:64 * par + 64, oc2, :],
                             y_sb[64 * par:64 * par + 64, oc2, :],
                             bcs_sb[64 * par:64 * par + 64, :], ALU.mult)))(),
                         reads=[b_bcs, b_y[oc2]], writes=[b_y[oc2]])

                wq_active = qb >= max(0, NQB - 6)
                for oc in range(8):
                    if wq_active:
                        wq_front()
                    pend_heads = list(range(2 * (oc - 2), 2 * (oc - 2) + 4)) if (oc % 2 == 0 and oc >= 2) else []
                    for p in range(npair):
                        emit_qk4(oc, p)
                        if p > 0:
                            emit_pv4(oc, p - 1)
                        emit_exp2(oc, p)
                        if p >= 1 and pend_heads:
                            emit_div_head(pend_heads.pop(0))
                    emit_pv4(oc, npair - 1)
                    while pend_heads:
                        emit_div_head(pend_heads.pop(0))
                    emit_ycopy(oc)
                    if oc % 2 == 1 and oc < 7:
                        emit_div_recip()
                    if wq_active:
                        wq_step()
                emit_div_recip()
                for head in range(12, 16):
                    emit_div_head(head)
                if qb == NQB - 1:
                    while wq_jobs or any(x is not None for x in wq_q[1:]):
                        wq_step()

                for tt in range(4):
                    ti = qb * 4 + tt
                    sl = tt % 2
                    P.dma("scalar", "xb%d" % sl, xA[:, sl, :], x_h[ti * 128:(ti + 1) * 128, :],
                          writes=[b_xA[sl]])
                    for half in range(2):
                        ps = (se_ps[:, 0:512], so_ps[:, 0:512])[half]
                        pb = (b_se, b_so)[half]
                        for oc in range(8):
                            P.op("tensor",
                                 (lambda ps=ps, oc=oc, tt=tt, half=half: (lambda h: h.matmul(
                                     ps, y_sb[:, oc, tt * 128:(tt + 1) * 128],
                                     wcp_sb[:, oc, half * 512:(half + 1) * 512],
                                     start=(oc == 0), stop=(oc == 7))))(),
                                 reads=[b_y[oc], b_wcp],
                                 wfirst=[pb] if oc == 0 else [], writes=[pb], wlast=(oc == 7))
                        P.op("vector",
                             (lambda ps=ps, sl=sl, half=half: (lambda h: h.tensor_tensor(
                                 xA[:, sl, half * 512:(half + 1) * 512], ps,
                                 xA[:, sl, half * 512:(half + 1) * 512], ALU.add)))(),
                             reads=[pb, b_xA[sl]], writes=[b_xA[sl]])
                    P.dma("scalar", "x2o%d" % sl, x2_h[ti * 128:(ti + 1) * 128, :], xA[:, sl, :],
                          reads=[b_xA[sl]], writes=[x2_bufs[ti]])

            P.barrier()
            cut = P.mark()
            _replay(nc, P, sems, {e: 0 for e in ENGS}, cut, waited_all)

        with contextlib.ExitStack() as c_stack:
            wfcT_r = c_stack.enter_context(nc.sbuf_tensor("wfcT_r", [128, 3, 8, 512], BF16))
            xC = c_stack.enter_context(nc.sbuf_tensor("xC", [128, 2, D], F32))
            ln2f = c_stack.enter_context(nc.sbuf_tensor("ln2f", [128, 2, D], F32))
            xqb = c_stack.enter_context(nc.sbuf_tensor("xqb", [128, 2, D], BF16))
            xqT = c_stack.enter_context(nc.sbuf_tensor("xqT", [128, 2, 8, 128], BF16))
            hf = c_stack.enter_context(nc.sbuf_tensor("hf", [128, 2, HID], F32))
            hqb = c_stack.enter_context(nc.sbuf_tensor("hqb", [128, 2, HID], BF16))
            hqT = c_stack.enter_context(nc.sbuf_tensor("hqT", [128, 32, 128], BF16))
            wfcT_res = c_stack.enter_context(nc.sbuf_tensor("wfcT_res", [128, 2, 8, 512], BF16))
            lnsmC = c_stack.enter_context(nc.sbuf_tensor("lnsmC", [128, 2, 8], F32))
            qsx = c_stack.enter_context(nc.sbuf_tensor("qsx", [128, 2, 4, 32], F32))
            qsh = c_stack.enter_context(nc.sbuf_tensor("qsh", [128, 2, 4, 128], F32))
            zh = c_stack.enter_context(nc.sbuf_tensor("zh", [128, 2, 512], F32))
            mh = c_stack.enter_context(nc.sbuf_tensor("mh", [128, 2, 512], I32))
            bfc_sb = c_stack.enter_context(nc.sbuf_tensor("bfc_sb", [1, HID], BF16))
            bpj_sb = c_stack.enter_context(nc.sbuf_tensor("bpj_sb", [1, D], BF16))
            psTC = c_stack.enter_context(nc.psum_tensor("psTC", [128, 1024], BF16))
            fc0 = c_stack.enter_context(nc.psum_tensor("fc0", [128, 512], F32))
            fc1 = c_stack.enter_context(nc.psum_tensor("fc1", [128, 512], F32))
            hT0 = c_stack.enter_context(nc.psum_tensor("hT0", [128, 1024], BF16))
            hT1 = c_stack.enter_context(nc.psum_tensor("hT1", [128, 1024], BF16))
            pj0 = c_stack.enter_context(nc.psum_tensor("pj0", [128, 512], F32))
            pj1 = c_stack.enter_context(nc.psum_tensor("pj1", [128, 512], F32))
            _dummy_c = 0
            b_wfcr = [Buf(), Buf(), Buf()]
            b_wfcres = [Buf(), Buf()]
            b_xC = [Buf(), Buf()]
            b_ln2 = [Buf(), Buf()]
            b_xq = [Buf(), Buf()]
            b_xqT = [Buf(), Buf()]
            b_hf = [[Buf() for _ in range(8)] for _ in range(2)]
            b_hq = [[Buf() for _ in range(8)] for _ in range(2)]
            b_hqT = [Buf()]
            b_lnC = [Buf(), Buf()]
            b_qsx = [Buf(), Buf()]
            b_qsh = [Buf(), Buf()]
            b_zh = [Buf(), Buf()]; b_mh = [Buf(), Buf()]
            b_bfc = Buf(); b_bpj = Buf()
            b_psTC = Buf()
            b_fc = [Buf(), Buf()]
            b_hT = [Buf(), Buf()]
            b_pj = [Buf(), Buf()]

            P.dma("gpsimd", "gbias", bfc_sb[:], bfc_h, writes=[b_bfc])
            ev_b = P.dma("gpsimd", "gbias", bpj_sb[:], bpj_h, writes=[b_bpj])
            b_bfc.w = ev_b

            def xquant_scales_thunks(t):
                am = qsx[:, t, 0, :]
                src3 = ln2f[:, t, :].rearrange("p (g e) -> p g e", e=32)
                return [
                    (lambda am=am, src3=src3, t=t: (lambda: P.op("vector", lambda h: h.tensor_reduce(
                            am.rearrange("p (g o) -> p g o", o=1), src3, AX.X, ALU.max,
                            apply_absolute_value=True),
                         reads=[b_ln2[t]], writes=[b_qsx[t]])))(),
                    (lambda am=am, t=t: (lambda: P.op("vector", lambda h: h.tensor_scalar_max(am, am, TINY),
                         reads=[b_qsx[t]], writes=[b_qsx[t]])))(),
                ] + scale_bits_thunks(b_qsx[t], qsx[:, t, :, :])

            def xquant_rounds(t):
                for ch in range(2):
                    sl = ch % 2
                    sc_b = qsx[:, t, 2, ch * 16:(ch + 1) * 16].to_broadcast([128, 16, 32])
                    rs_b = qsx[:, t, 3, ch * 16:(ch + 1) * 16].to_broadcast([128, 16, 32])
                    src3 = ln2f[:, t, ch * 512:(ch + 1) * 512].rearrange("p (g e) -> p g e", e=32)
                    dst3 = xqb[:, t, ch * 512:(ch + 1) * 512].rearrange("p (g e) -> p g e", e=32)
                    P.op("vector", (lambda src3=src3, rs_b=rs_b, sl=sl: (lambda h: h.tensor_tensor(
                            zh[:, sl, :].rearrange("p (g e) -> p g e", e=32), src3, rs_b, ALU.mult)))(),
                         reads=[b_ln2[t], b_qsx[t]], writes=[b_zh[sl]])
                    P.op("vector", (lambda sl=sl: (lambda h: h.tensor_copy(mh[:, sl, :], zh[:, sl, :])))(),
                         reads=[b_zh[sl]], writes=[b_mh[sl]])
                    P.op("vector", (lambda dst3=dst3, sc_b=sc_b, sl=sl: (lambda h: h.tensor_tensor(
                            dst3, mh[:, sl, :].rearrange("p (g e) -> p g e", e=32), sc_b, ALU.mult)))(),
                         reads=[b_mh[sl], b_qsx[t]], writes=[b_xq[t]])

            def xqT_emit(t):
                for j in range(8):
                    P.op("tensor",
                         (lambda t=t, j=j: (lambda h: h.matmul(
                             psTC[:, j * 128:(j + 1) * 128], xqb[:, t, j * 128:(j + 1) * 128],
                             ident_sb[:], start=(j == 0), stop=(j == 7), is_transpose=True)))(),
                         reads=[b_xq[t], b_ident],
                         wfirst=[b_psTC] if j == 0 else [], writes=[b_psTC], wlast=(j == 7))
                P.op("scalar",
                     (lambda t=t: (lambda h: h.activation(
                         xqT[:, t, :, :], psTC[:].rearrange("p (c q) -> p c q", q=128), AF.Identity)))(),
                     reads=[b_psTC], writes=[b_xqT[t]])

            def front_a_loads(st):
                for t in range(2):
                    ti = 2 * st + t
                    P.dma("scalar", "x2i%d" % t, xC[:, t, :], x2_h[ti * 128:(ti + 1) * 128, :],
                          extra=[x2_bufs[ti].w], writes=[b_xC[t]])

            def _fa_chains():
                return [layernorm_thunks(b_xC[t], xC[:, t, :], ln2f[:, t, :], b_ln2[t],
                                         lnsmC[:, t, :], b_lnC[t], ln2f[:, t, :], b_ln2[t],
                                         sq_gpsimd=True)
                        + xquant_scales_thunks(t) for t in range(2)]

            def fa_v1(st):
                interleave(*[c[0:7] for c in _fa_chains()])

            def fa_s(st):
                interleave(*[c[7:11] for c in _fa_chains()])

            def fa_v2(st):
                interleave(*[c[11:] for c in _fa_chains()])
                for t in range(2):
                    xquant_rounds(t)

            def front_a_T(st):
                emit_wld(2)
                emit_wld(3)
                emit_wld(4)
                for t in range(2):
                    xqT_emit(t)

            def emit_wld(hb):
                sl = hb % 3
                ld_extra = [b.w for b in wfcT_bufs]
                P.dma("sync", "wld%d" % sl, wfcT_r[:, sl, :, :],
                      wfcT_h[:, :, hb * 512:(hb + 1) * 512].rearrange("f p q -> p f q"),
                      extra=ld_extra, writes=[b_wfcr[sl]])

            def front_b(st):
                for hb in range(8):
                    if hb < 2:
                        wsrc, wbuf = wfcT_res[:, hb, :, :], b_wfcres[hb]
                    else:
                        wsrc, wbuf = wfcT_r[:, hb % 3, :, :], b_wfcr[hb % 3]
                    for t in range(2):
                        ps = (fc0, fc1)[t]
                        pb = b_fc[t]
                        for f in range(8):
                            P.op("tensor",
                                 (lambda ps=ps, t=t, f=f, wsrc=wsrc: (lambda h: h.matmul(
                                     ps[:], xqT[:, t, f, :], wsrc[:, f, :],
                                     start=(f == 0), stop=False)))(),
                                 reads=[b_xqT[t], wbuf],
                                 wfirst=[pb] if f == 0 else [], writes=[pb], wlast=False)
                        P.op("tensor",
                             (lambda ps=ps, hb=hb: (lambda h: h.matmul(
                                 ps[:], ones_sb[0:1, 0:128], bfc_sb[0:1, hb * 512:(hb + 1) * 512],
                                 start=False, stop=True)))(),
                             reads=[b_ones, b_bfc], writes=[pb])
                        P.op("scalar",
                             (lambda ps=ps, t=t, hb=hb: (lambda h: h.activation(
                                 hf[:, t, hb * 512:(hb + 1) * 512], ps[:], AF.Gelu)))(),
                             reads=[pb], writes=[b_hf[t][hb]])
                    if 2 <= hb < 5:
                        emit_wld(hb + 3)

            def tail_amax(st):
                for hb in range(8):
                    for t in range(2):
                        P.op("vector",
                             (lambda t=t, hb=hb: (lambda h: h.tensor_reduce(
                                 qsh[:, t, 0, hb * 16:(hb + 1) * 16].rearrange("p (g o) -> p g o", o=1),
                                 hf[:, t, hb * 512:(hb + 1) * 512].rearrange("p (g e) -> p g e", e=32),
                                 AX.X, ALU.max, apply_absolute_value=True)))(),
                             reads=[b_hf[t][hb]], writes=[b_qsh[t]])

            def stage_tail(st):
                for t in range(2):
                    am = qsh[:, t, 0, :]
                    P.op("vector", (lambda am=am: (lambda h: h.tensor_scalar_max(am, am, TINY)))(),
                         reads=[b_qsh[t]], writes=[b_qsh[t]])
                    for th in scale_bits_thunks(b_qsh[t], qsh[:, t, :, :]):
                        th()
                for hb in range(8):
                    for t in range(2):
                        sc_b = qsh[:, t, 2, hb * 16:(hb + 1) * 16].to_broadcast([128, 16, 32])
                        rs_b = qsh[:, t, 3, hb * 16:(hb + 1) * 16].to_broadcast([128, 16, 32])
                        src3 = hf[:, t, hb * 512:(hb + 1) * 512].rearrange("p (g e) -> p g e", e=32)
                        dst3 = hqb[:, t, hb * 512:(hb + 1) * 512].rearrange("p (g e) -> p g e", e=32)
                        rsl = t
                        P.op("gpsimd", (lambda src3=src3, rs_b=rs_b, rsl=rsl: (lambda h: h.tensor_tensor(
                                zh[:, rsl, :].rearrange("p (g e) -> p g e", e=32), src3, rs_b, ALU.mult)))(),
                             reads=[b_hf[t][hb], b_qsh[t]], writes=[b_zh[rsl]])
                        P.op("vector", (lambda rsl=rsl: (lambda h: h.tensor_copy(mh[:, rsl, :], zh[:, rsl, :])))(),
                             reads=[b_zh[rsl]], writes=[b_mh[rsl]])
                        P.op("vector", (lambda dst3=dst3, sc_b=sc_b, t=t, rsl=rsl: (lambda h: h.tensor_tensor(
                                dst3, mh[:, rsl, :].rearrange("p (g e) -> p g e", e=32), sc_b, ALU.mult)))(),
                             reads=[b_mh[rsl], b_qsh[t]], writes=[b_hq[t][hb]])

            def _hqT_emit(st, t):
                for grp in range(4):
                        hps = (hT0, hT1)[grp % 2]
                        hpb = b_hT[grp % 2]
                        for j in range(8):
                            hc = grp * 8 + j
                            P.op("tensor",
                                 (lambda hps=hps, t=t, hc=hc, j=j: (lambda h: h.matmul(
                                     hps[:, j * 128:(j + 1) * 128], hqb[:, t, hc * 128:(hc + 1) * 128],
                                     ident_sb[:], start=(j == 0), stop=(j == 7), is_transpose=True)))(),
                                 reads=[b_hq[t][hc // 4], b_ident],
                                 wfirst=[hpb] if j == 0 else [], writes=[hpb], wlast=(j == 7))
                        P.op("scalar",
                             (lambda hps=hps, t=t, grp=grp: (lambda h: h.activation(
                                 hqT[:, grp * 8:(grp + 1) * 8, :],
                                 hps[:].rearrange("p (c q) -> p c q", q=128), AF.Identity)))(),
                             reads=[hpb], writes=[b_hqT[0]])

            def stage_proj_T(st):
                _hqT_emit(st, 0)

            def stage_proj_rest(st):
                for t in range(2):
                    ti = 2 * st + t
                    P.dma("sync", "x2i%d" % (2 + t), xC[:, t, :], x2_h[ti * 128:(ti + 1) * 128, :],
                          extra=[x2_bufs[ti].w], writes=[b_xC[t]])
                    if t == 1:
                        _hqT_emit(st, 1)
                    for half in range(2):
                        ps = (pj0, pj1)[half]
                        pb = b_pj[half]
                        for hc in range(32):
                            P.op("tensor",
                                 (lambda ps=ps, t=t, hc=hc, half=half: (lambda h: h.matmul(
                                     ps[:], hqT[:, hc, :], wpjT[:, hc, half * 512:(half + 1) * 512],
                                     start=(hc == 0), stop=False)))(),
                                 reads=[b_hqT[0], b_wpjT],
                                 wfirst=[pb] if hc == 0 else [], writes=[pb], wlast=False)
                        P.op("tensor",
                             (lambda ps=ps, half=half: (lambda h: h.matmul(
                                 ps[:], ones_sb[0:1, 0:128], bpj_sb[0:1, half * 512:(half + 1) * 512],
                                 start=False, stop=True)))(),
                             reads=[b_ones, b_bpj], writes=[pb])
                        P.op("vector",
                             (lambda ps=ps, t=t, half=half: (lambda h: h.tensor_tensor(
                                 xC[:, t, half * 512:(half + 1) * 512], ps[:],
                                 xC[:, t, half * 512:(half + 1) * 512], ALU.add)))(),
                             reads=[pb, b_xC[t]], writes=[b_xC[t]])
                    P.dma("scalar", "oo%d" % t, out_h[ti * 128:(ti + 1) * 128, :], xC[:, t, :],
                          reads=[b_xC[t]])

            for hb in range(2):
                P.dma("sync", "wre%d" % hb, wfcT_res[:, hb, :, :],
                      wfcT_h[:, :, hb * 512:(hb + 1) * 512].rearrange("f p q -> p f q"),
                      extra=[b.w for b in wfcT_bufs], writes=[b_wfcres[hb]])
            def front_a_all(st):
                fa_v1(st)
                fa_s(st)
                fa_v2(st)

            front_a_loads(0)
            front_a_all(0)
            front_a_T(0)
            if NST > 1:
                front_a_loads(1)
                front_a_all(1)
            for st in range(NST):
                if st + 2 < NST:
                    front_a_loads(st + 2)
                    fa_v1(st + 2)
                front_b(st)
                if st + 1 < NST:
                    front_a_T(st + 1)
                tail_amax(st)
                if st > 0:
                    stage_proj_T(st - 1)
                if st + 2 < NST:
                    fa_s(st + 2)
                    fa_v2(st + 2)
                if st > 0:
                    stage_proj_rest(st - 1)
                stage_tail(st)
            stage_proj_T(NST - 1)
            stage_proj_rest(NST - 1)

            P.items["sync"].append(("op", (lambda h: h.nop()),
                                    (("oo0", P.dma_cnt["oo0"]), ("oo1", P.dma_cnt["oo1"]))))
            P.cnt["sync"] += 1

            end = P.mark()
            _replay(nc, P, sems, cut, end, waited_all)

    return nc


_NC_CACHE = {}


def _get_nc(T):
    if T not in _NC_CACHE:
        _NC_CACHE[T] = build_program(T)
    return _NC_CACHE[T]


def kernel(L0_attn_x, L0_attn_Wc_down, L0_attn_Wq_up, L0_attn_Wkv_up, L0_attn_Wc_proj,
           L0_ffn_w_fc, L0_ffn_b_fc, L0_ffn_w_proj, L0_ffn_b_proj):
    x = np.ascontiguousarray(np.asarray(L0_attn_x, dtype=np.float32))
    B, T, _ = x.shape
    nc = _get_nc(T)
    ident = np.eye(128, dtype=np.float32)
    tri = np.triu(np.ones((128, 128), dtype=np.float32))
    sel = np.zeros((128, 256), dtype=np.float32)
    for m in range(4):
        sel[32 * m, m * 64:(m + 1) * 64] = 1.0
    common = {
        "wdn": np.ascontiguousarray(L0_attn_Wc_down, dtype=np.float32),
        "wqu": np.ascontiguousarray(L0_attn_Wq_up, dtype=np.float32),
        "wkv": np.ascontiguousarray(L0_attn_Wkv_up, dtype=np.float32),
        "wcp": np.ascontiguousarray(L0_attn_Wc_proj, dtype=np.float32),
        "wfc": np.ascontiguousarray(L0_ffn_w_fc, dtype=np.float32),
        "bfc": np.ascontiguousarray(np.asarray(L0_ffn_b_fc, dtype=np.float32).reshape(1, -1)),
        "wpj": np.ascontiguousarray(L0_ffn_w_proj, dtype=np.float32),
        "bpj": np.ascontiguousarray(np.asarray(L0_ffn_b_proj, dtype=np.float32).reshape(1, -1)),
        "ident": ident,
        "tri": tri,
        "sel": sel,
    }
    in_maps = [dict(common, x=np.ascontiguousarray(x[i])) for i in range(B)]
    res = run_bass_kernel_spmd(nc, in_maps, core_ids=list(range(B)))
    return np.stack([np.asarray(r["out"], dtype=np.float32) for r in res.results], axis=0)


# version 67
# speedup vs baseline: 1.0013x; 1.0013x over previous
import contextlib
import numpy as np
import concourse.bass as bass
import concourse.mybir as mybir
from concourse.bass_utils import run_bass_kernel_spmd

F32 = mybir.dt.float32
I32 = mybir.dt.int32
BF16 = mybir.dt.bfloat16
AF = mybir.ActivationFunctionType
ALU = mybir.AluOpType
AX = mybir.AxisListType
LN2C = float(np.log(2.0))
TINY = float(2.0 ** -126)

ENGS = ("tensor", "vector", "scalar", "gpsimd", "sync")

D = 1024
NH = 16
HD = 64
LAT = 512
HID = 4096
QB = 512


class Buf:
    __slots__ = ("w", "rs")

    def __init__(self):
        self.w = None
        self.rs = []


class Prog:

    def __init__(self):
        self.items = {e: [] for e in ENGS}
        self.cnt = {e: 0 for e in ENGS}
        self.dma_cnt = {}

    @staticmethod
    def _collect(reads, wbufs, extra):
        deps = []
        for b in reads:
            if b.w is not None:
                deps.append(b.w)
        for b in wbufs:
            if b.w is not None:
                deps.append(b.w)
            deps.extend(b.rs)
        deps.extend(e for e in extra if e is not None)
        return deps

    def op(self, eng, fn, reads=(), writes=(), extra=(), wfirst=None, wlast=True):
        wbufs = list(writes) if wfirst is None else list(wfirst)
        deps = self._collect(reads, wbufs, extra) + self.take_pending(eng)
        self.items[eng].append(("op", fn, tuple(deps)))
        self.cnt[eng] += 1
        ev = (eng, self.cnt[eng])
        for b in reads:
            b.rs.append(ev)
        if wlast:
            for b in writes:
                b.w = ev
                b.rs = []
        return ev

    def dma(self, qeng, sem, out_ap, in_ap, reads=(), writes=(), extra=()):
        deps = self._collect(reads, writes, extra) + self.take_pending(qeng)
        self.dma_cnt[sem] = self.dma_cnt.get(sem, 0) + 16
        val = self.dma_cnt[sem]
        self.items[qeng].append(("dma", (out_ap, in_ap, sem), tuple(deps)))
        ev = (sem, val)
        for b in reads:
            b.rs.append(ev)
        for b in writes:
            b.w = ev
            b.rs = []
        return ev

    def barrier(self):
        evs = [(e, self.cnt[e]) for e in ENGS]
        evs += [(s, v) for s, v in self.dma_cnt.items()]
        self.pending = {e: [ev for ev in evs if ev[0] != e and ev[1] > 0] for e in ENGS}

    def take_pending(self, eng):
        p = getattr(self, "pending", None)
        if p and eng in p:
            return p.pop(eng)
        return []

    def mark(self):
        return {e: len(self.items[e]) for e in ENGS}


def _replay(nc, P, sems, lo, hi, waited_all):
    with nc.Block() as blk:
        def mk(eng):
            def body(handle):
                waited = waited_all[eng]
                for kind, a, deps in P.items[eng][lo[eng]:hi[eng]]:
                    for (sn, val) in deps:
                        if waited.get(sn, 0) >= val:
                            continue
                        handle.wait_ge(sems[sn], val)
                        waited[sn] = val
                    if kind == "op":
                        a(handle).then_inc(sems[eng], 1)
                    else:
                        out_ap, in_ap, sn = a
                        handle.dma_start(out=out_ap, in_=in_ap).then_inc(sems[sn], 16)
            return body
        blk.tensor(mk("tensor"))
        blk.vector(mk("vector"))
        blk.scalar(mk("scalar"))
        blk.gpsimd(mk("gpsimd"))
        blk.sync(mk("sync"))


def build_program(T):
    NT = T // 128
    NQB = T // QB
    NG = NT // 4
    NST = NT // 2

    nc = bass.Bass("TRN2", target_bir_lowering=False)
    P = Prog()

    x_h = nc.dram_tensor("x", [T, D], F32, kind="ExternalInput").ap()
    wdn_h = nc.dram_tensor("wdn", [D, LAT], F32, kind="ExternalInput").ap()
    wqu_h = nc.dram_tensor("wqu", [LAT, D], F32, kind="ExternalInput").ap()
    wkv_h = nc.dram_tensor("wkv", [LAT, 2 * HD], F32, kind="ExternalInput").ap()
    wcp_h = nc.dram_tensor("wcp", [D, D], F32, kind="ExternalInput").ap()
    wfc_h = nc.dram_tensor("wfc", [HID, D], F32, kind="ExternalInput").ap()
    bfc_h = nc.dram_tensor("bfc", [1, HID], F32, kind="ExternalInput").ap()
    wpj_h = nc.dram_tensor("wpj", [D, HID], F32, kind="ExternalInput").ap()
    bpj_h = nc.dram_tensor("bpj", [1, D], F32, kind="ExternalInput").ap()
    ident_h = nc.dram_tensor("ident", [128, 128], F32, kind="ExternalInput").ap()
    tri_h = nc.dram_tensor("tri", [128, 128], F32, kind="ExternalInput").ap()
    sel_h = nc.dram_tensor("sel", [128, 256], F32, kind="ExternalInput").ap()
    x2_h = nc.dram_tensor("x2s", [T, D], F32, kind="Internal").ap()
    wfcT_h = nc.dram_tensor("wfcTs", [8, 128, HID], BF16, kind="Internal").ap()
    lat_h = nc.dram_tensor("lats", [4, 128, T], BF16, kind="Internal").ap()
    out_h = nc.dram_tensor("out", [T, D], F32, kind="ExternalOutput").ap()

    x2_bufs = [Buf() for _ in range(NT)]
    wfcT_bufs = [Buf() for _ in range(32)]

    dma_sem_names = [
        "xa0", "xa1", "xb0", "xb1", "x2o0", "x2o1", "x2i0", "x2i1", "x2i2", "x2i3",
        "oo0", "oo1", "wsp0", "wsp1", "wld0", "wld1", "wld2", "kf2s",
        "gwin", "wfq0", "wfq1", "wpq0", "wpq1", "gbias", "lsp", "lld0", "lld1", "wre0", "wre1",
    ]

    with contextlib.ExitStack() as stack:
        ec = stack.enter_context
        ident_sb = ec(nc.sbuf_tensor("ident_sb", [128, 128], BF16))
        tri_sb = ec(nc.sbuf_tensor("tri_sb", [128, 128], BF16))
        ones_sb = ec(nc.sbuf_tensor("ones_sb", [1, 128], BF16))
        wpjT = ec(nc.sbuf_tensor("wpjT", [128, 32, D], BF16))
        wqt_ps = ec(nc.psum_tensor("wqt_ps", [128, 1024], BF16))
        sems = {e: ec(nc.semaphore("s_" + e)) for e in ENGS}
        for n_ in dma_sem_names:
            sems[n_] = ec(nc.semaphore(n_))

        waited_all = {e: {} for e in ENGS}
        b_ident = Buf(); b_tri = Buf(); b_ones = Buf()
        b_wpjT = Buf(); b_wqt = Buf()

        with contextlib.ExitStack() as ab_stack:
            latg = ab_stack.enter_context(nc.sbuf_tensor("latg", [128, 4, QB], BF16))
            latB = ab_stack.enter_context(nc.sbuf_tensor("latB", [128, 1, 4, QB], BF16))
            kf = ab_stack.enter_context(nc.sbuf_tensor("kf", [128, T], BF16))
            v_tm = ab_stack.enter_context(nc.sbuf_tensor("v_tm", [128, NT, 66], BF16))
            wdn_sb = ab_stack.enter_context(nc.sbuf_tensor("wdn_sb", [128, 8, LAT], BF16))
            wqu_sb = ab_stack.enter_context(nc.sbuf_tensor("wqu_sb", [128, 4, D], BF16))
            wkv_sb = ab_stack.enter_context(nc.sbuf_tensor("wkv_sb", [128, 4, 2 * HD], BF16))
            wcp_sb = ab_stack.enter_context(nc.sbuf_tensor("wcp_sb", [128, 8, D], BF16))
            xA = ab_stack.enter_context(nc.sbuf_tensor("xA", [128, 2, D], F32))
            ln1_sb = ab_stack.enter_context(nc.sbuf_tensor("ln1_sb", [128, 2, D], BF16))
            ln1T = ab_stack.enter_context(nc.sbuf_tensor("ln1T", [128, 8, QB], BF16))
            kv_sb = ab_stack.enter_context(nc.sbuf_tensor("kv_sb", [128, QB], BF16))
            q_sb = ab_stack.enter_context(nc.sbuf_tensor("q_sb", [128, 8, QB], BF16))
            pT = ab_stack.enter_context(nc.sbuf_tensor("pT", [128, 2, 2, 2 * QB], BF16))
            sel_sb = ab_stack.enter_context(nc.sbuf_tensor("sel_sb", [128, 256], BF16))
            dens_sb = ab_stack.enter_context(nc.sbuf_tensor("dens", [128, QB], F32))
            recba_sb = ab_stack.enter_context(nc.sbuf_tensor("recba", [128, QB], BF16))
            y_sb = ab_stack.enter_context(nc.sbuf_tensor("y_sb", [128, 8, QB], BF16))
            bcs_sb = ab_stack.enter_context(nc.sbuf_tensor("bcs", [128, QB], BF16))
            lnsm = ab_stack.enter_context(nc.sbuf_tensor("lnsm", [128, 2, 8], F32))
            wqf = ab_stack.enter_context(nc.sbuf_tensor("wqf", [128, 2, D], F32))
            wqb = ab_stack.enter_context(nc.sbuf_tensor("wqb", [128, D], BF16))
            wstg = ab_stack.enter_context(nc.sbuf_tensor("wstg", [128, 8, 128], BF16))
            wpf = ab_stack.enter_context(nc.sbuf_tensor("wpf", [128, 2, HID // 2], F32))
            wpb = ab_stack.enter_context(nc.sbuf_tensor("wpb", [128, HID // 2], BF16))
            qs1 = ab_stack.enter_context(nc.sbuf_tensor("qs1", [128, 2, 4, 32], F32))
            qs2 = ab_stack.enter_context(nc.sbuf_tensor("qs2", [128, 2, 4, 64], F32))
            zq = ab_stack.enter_context(nc.sbuf_tensor("zq", [128, 512], F32))
            mq = ab_stack.enter_context(nc.sbuf_tensor("mq", [128, 512], I32))
            qi1 = ab_stack.enter_context(nc.sbuf_tensor("qi1", [128, 64], I32))
            bc_ps = ab_stack.enter_context(nc.psum_tensor("bc_ps", [64, 512], F32))
            se_ps = ab_stack.enter_context(nc.psum_tensor("se_ps", [128, 1024], F32))
            so_ps = ab_stack.enter_context(nc.psum_tensor("so_ps", [128, 1024], F32))
            y0 = ab_stack.enter_context(nc.psum_tensor("y0", [128, 512], F32))
            y1 = ab_stack.enter_context(nc.psum_tensor("y1", [128, 512], F32))
            _dummy_ab = 0
            b_latg = Buf()
            b_latB = [Buf()]
            lat_bufs = [Buf() for _ in range(64)]
            b_kf = Buf(); b_kf2 = Buf()
            b_vtm = [Buf() for _ in range(NT)]
            b_wdn = Buf(); b_wqu = Buf(); b_wkv = Buf(); b_wcp = Buf()
            b_xA = [Buf(), Buf()]
            b_ln1 = [Buf(), Buf()]
            b_ln1T = Buf()
            b_kvsb = Buf()
            b_q = [Buf() for _ in range(8)]
            b_pT = [[Buf(), Buf()], [Buf(), Buf()]]
            b_sel = Buf(); b_dens = Buf(); b_recba = Buf()
            b_y = [Buf() for _ in range(8)]
            b_bcs = Buf()
            b_lnsm = [Buf(), Buf()]
            b_wqf = [Buf(), Buf()]
            b_wqb = Buf(); b_wstg = Buf()
            b_wpf = [Buf(), Buf()]
            b_wpb = Buf()
            b_qs1 = [Buf(), Buf()]; b_qs2 = [Buf(), Buf()]
            b_zq = Buf(); b_mq = Buf(); b_qi1 = Buf()
            b_bc = Buf()
            b_s = [[Buf(), Buf()], [Buf(), Buf()]]
            b_se = b_s[0][0]; b_so = b_s[1][0]
            b_y_ps = [Buf(), Buf()]

            P.dma("gpsimd", "gwin", ident_sb[:], ident_h, writes=[b_ident])
            P.dma("gpsimd", "gwin", tri_sb[:], tri_h, writes=[b_tri])
            P.dma("gpsimd", "gwin", sel_sb[:], sel_h, writes=[b_sel])
            P.dma("gpsimd", "gwin", wdn_sb[:], wdn_h.rearrange("(c p) m -> p c m", p=128), writes=[b_wdn])
            P.dma("gpsimd", "gwin", wqu_sb[:], wqu_h.rearrange("(c p) m -> p c m", p=128), writes=[b_wqu])
            P.dma("gpsimd", "gwin", wkv_sb[:], wkv_h.rearrange("(c p) m -> p c m", p=128), writes=[b_wkv])
            gw_total = P.dma("gpsimd", "gwin", wcp_sb[:], wcp_h.rearrange("(c p) m -> p c m", p=128), writes=[b_wcp])
            for b in (b_ident, b_tri, b_sel, b_wdn, b_wqu, b_wkv, b_wcp):
                b.w = gw_total

            P.op("vector", lambda h: h.memset(ones_sb[:], 1.0), writes=[b_ones])
            P.op("vector", lambda h: h.memset(dens_sb[:], 1.0), writes=[b_dens])
            ev_vones = P.op("vector", lambda h: h.memset(v_tm[:, :, 64:65], 1.0))
            for b in b_vtm:
                b.w = ev_vones

            def layernorm_thunks(xbuf, x_ap, out_ap, out_buf, sm, smb, ztmp, zbuf, ztmp_w=None,
                                 sq_gpsimd=False):
                if ztmp_w is None:
                    sq = [
                        (lambda: P.op("gpsimd", lambda h: h.tensor_tensor(ztmp, x_ap, x_ap, ALU.mult),
                                      reads=[xbuf], writes=[zbuf])) if sq_gpsimd else
                        (lambda: P.op("scalar", lambda h: h.activation(ztmp, x_ap, AF.Square),
                                      reads=[xbuf], writes=[zbuf])),
                        lambda: P.op("vector", lambda h: h.reduce_sum(sm[:, 1:2], ztmp, AX.X),
                                     reads=[zbuf], writes=[smb]),
                    ]
                else:
                    sq = [
                        lambda: P.op("scalar", lambda h: h.activation(ztmp, x_ap[:, 0:ztmp_w], AF.Square),
                                     reads=[xbuf], writes=[zbuf]),
                        lambda: P.op("vector", lambda h: h.reduce_sum(sm[:, 1:2], ztmp, AX.X),
                                     reads=[zbuf], writes=[smb]),
                        lambda: P.op("scalar", lambda h: h.activation(ztmp, x_ap[:, ztmp_w:D], AF.Square),
                                     reads=[xbuf], writes=[zbuf]),
                        lambda: P.op("vector", lambda h: h.reduce_sum(sm[:, 3:4], ztmp, AX.X),
                                     reads=[zbuf], writes=[smb]),
                        lambda: P.op("vector", lambda h: h.tensor_tensor(sm[:, 1:2], sm[:, 1:2], sm[:, 3:4], ALU.add),
                                     reads=[smb], writes=[smb]),
                    ]
                return [
                    lambda: P.op("vector", lambda h: h.reduce_sum(sm[:, 0:1], x_ap, AX.X),
                                 reads=[xbuf], writes=[smb]),
                ] + sq + [
                    lambda: P.op("vector", lambda h: h.tensor_scalar_mul(sm[:, 2:3], sm[:, 0:1], 1.0 / D),
                                 reads=[smb], writes=[smb]),
                    lambda: P.op("vector", lambda h: h.tensor_tensor(sm[:, 3:4], sm[:, 2:3], sm[:, 2:3], ALU.mult),
                                 reads=[smb], writes=[smb]),
                    lambda: P.op("vector", lambda h: h.scalar_tensor_tensor(sm[:, 4:5], sm[:, 1:2], 1.0 / D, sm[:, 3:4], ALU.mult, ALU.subtract),
                                 reads=[smb], writes=[smb]),
                    lambda: P.op("vector", lambda h: h.tensor_scalar_add(sm[:, 4:5], sm[:, 4:5], 1e-5),
                                 reads=[smb], writes=[smb]),
                    lambda: P.op("scalar", lambda h: h.activation(sm[:, 5:6], sm[:, 4:5], AF.Ln),
                                 reads=[smb], writes=[smb]),
                    lambda: P.op("scalar", lambda h: h.activation(sm[:, 6:7], sm[:, 5:6], AF.Exp, scale=-0.5),
                                 reads=[smb], writes=[smb]),
                    lambda: P.op("vector", lambda h: h.scalar_tensor_tensor(sm[:, 7:8], sm[:, 2:3], -1.0, sm[:, 6:7], ALU.mult, ALU.mult),
                                 reads=[smb], writes=[smb]),
                    lambda: P.op("scalar", lambda h: h.activation(out_ap, x_ap, AF.Identity, bias=sm[:, 7:8], scale=sm[:, 6:7]),
                                 reads=[xbuf, smb], writes=[out_buf]),
                ]

            def layernorm(xbuf, x_ap, out_ap, out_buf, sm, smb, ztmp, zbuf, ztmp_w=None):
                for th in layernorm_thunks(xbuf, x_ap, out_ap, out_buf, sm, smb, ztmp, zbuf, ztmp_w):
                    th()

            def interleave(*lists):
                n = max(len(l) for l in lists)
                for i in range(n):
                    for l in lists:
                        if i < len(l):
                            l[i]()

            def a_ln_pair(p):
                chains = []
                for sl in range(2):
                    ti = 2 * p + sl
                    P.dma("sync", "xa%d" % sl, xA[:, sl, :], x_h[ti * 128:(ti + 1) * 128, :],
                          writes=[b_xA[sl]])
                    ztmp = (zq[:], dens_sb[:])[sl]
                    zbuf = (b_zq, b_dens)[sl]
                    chains.append(layernorm_thunks(
                        b_xA[sl], xA[:, sl, :], ln1_sb[:, sl, :], b_ln1[sl],
                        lnsm[:, sl, :], b_lnsm[sl], ztmp, zbuf, ztmp_w=512))
                interleave(*chains)

            def a_trans_pair(p):
                for sl in range(2):
                    i = 2 * (p % 2) + sl
                    for j in range(8):
                        P.op("tensor",
                             (lambda sl=sl, j=j: (lambda h: h.matmul(
                                 wqt_ps[:, j * 128:(j + 1) * 128], ln1_sb[:, sl, j * 128:(j + 1) * 128],
                                 ident_sb[:], start=(j == 0), stop=(j == 7), is_transpose=True)))(),
                             reads=[b_ln1[sl], b_ident],
                             wfirst=[b_wqt] if j == 0 else [], writes=[b_wqt], wlast=(j == 7))
                    P.op("vector",
                         (lambda i=i: (lambda h: h.tensor_copy(
                             ln1T[:, :, i * 128:(i + 1) * 128],
                             wqt_ps[:].rearrange("p (c q) -> p c q", q=128))))(),
                         reads=[b_wqt], writes=[b_ln1T])

            def a_latent_kv(g):
                for lc in range(4):
                    ps = (se_ps[:, 0:512], se_ps[:, 512:1024], so_ps[:, 0:512], so_ps[:, 512:1024])[lc]
                    pb = (b_s[0][0], b_s[0][1], b_s[1][0], b_s[1][1])[lc]
                    for f in range(8):
                        P.op("tensor",
                             (lambda lc=lc, f=f, ps=ps: (lambda h: h.matmul(
                                 ps, wdn_sb[:, f, lc * 128:(lc + 1) * 128], ln1T[:, f, :],
                                 start=(f == 0), stop=(f == 7))))(),
                             reads=[b_wdn, b_ln1T],
                             wfirst=[pb] if f == 0 else [], writes=[pb], wlast=(f == 7))
                    P.op("vector",
                         (lambda lc=lc, ps=ps: (lambda h: h.tensor_copy(
                             latg[:, lc, :], ps)))(),
                         reads=[pb], writes=[b_latg])
                for l in range(4):
                    P.op("tensor",
                         (lambda l=l: (lambda h: h.matmul(
                             se_ps[:, 0:512], wkv_sb[:, l, :], latg[:, l, :],
                             start=(l == 0), stop=(l == 3))))(),
                         reads=[b_wkv, b_latg],
                         wfirst=[b_se] if l == 0 else [], writes=[b_se], wlast=(l == 3))
                P.dma("sync", "lsp", lat_h[:, :, g * QB:(g + 1) * QB].rearrange("c p q -> p c q"),
                      latg[:], reads=[b_latg], writes=[lat_bufs[g]])
                P.op("vector", lambda h: h.tensor_copy(kv_sb[:], se_ps[:, 0:512]),
                     reads=[b_se], writes=[b_kvsb])
                P.op("vector",
                     (lambda g=g: (lambda h: h.tensor_copy(kf[0:64, g * QB:(g + 1) * QB], kv_sb[0:64, :])))(),
                     reads=[b_kvsb], writes=[b_kf])
                for j in range(4):
                    P.op("tensor",
                         (lambda j=j: (lambda h: h.matmul(
                             wqt_ps[:, j * 128:(j + 1) * 128], kv_sb[:, j * 128:(j + 1) * 128],
                             ident_sb[:], start=(j == 0), stop=(j == 3), is_transpose=True)))(),
                         reads=[b_kvsb, b_ident],
                         wfirst=[b_wqt] if j == 0 else [], writes=[b_wqt], wlast=(j == 3))
                for j in range(4):
                    ti = 4 * g + j
                    P.op("vector",
                         (lambda j=j, ti=ti: (lambda h: h.tensor_copy(
                             v_tm[:, ti, 0:64], wqt_ps[:, j * 128 + 64:(j + 1) * 128])))(),
                         reads=[b_wqt], writes=[b_vtm[ti]])

            a_ln_pair(0)
            for p in range(2 * NG):
                a_trans_pair(p)
                if p + 1 < 2 * NG:
                    a_ln_pair(p + 1)
                if p % 2 == 1:
                    a_latent_kv(p // 2)
            P.dma("sync", "kf2s", kf[64:128, :], kf[0:64, :], reads=[b_kf], writes=[b_kf2])
            P.op("vector", lambda h: h.memset(dens_sb[:], 1.0), writes=[b_dens])

            def quant_amax(src_buf, src_ap, sm_buf, sm_ap, G):
                src3 = src_ap.rearrange("p (g e) -> p g e", e=32)
                am = sm_ap[:, 0, :]
                P.op("vector", (lambda am=am, src3=src3: (lambda h: h.tensor_reduce(
                        am.rearrange("p (g o) -> p g o", o=1), src3, AX.X, ALU.max,
                        apply_absolute_value=True)))(),
                     reads=[src_buf], writes=[sm_buf])
                P.op("gpsimd", (lambda am=am: (lambda h: h.tensor_scalar_max(am, am, TINY)))(),
                     reads=[sm_buf], writes=[sm_buf])

            def scale_bits_thunks(sm_buf, sm_ap):
                am_i = sm_ap[:, 0, :].bitcast(I32)
                sc = sm_ap[:, 2, :]; rs = sm_ap[:, 3, :]
                sc_i = sc.bitcast(I32)
                return [
                    (lambda sc_i=sc_i, am_i=am_i, sm_buf=sm_buf: (lambda: P.op(
                        "vector", lambda h: h.tensor_scalar(
                            sc_i, am_i, 0x7F800000, None, ALU.bitwise_and),
                        reads=[sm_buf], writes=[sm_buf])))(),
                    (lambda sc=sc, sm_buf=sm_buf: (lambda: P.op(
                        "vector", lambda h: h.tensor_scalar_mul(sc, sc, 0.125),
                        reads=[sm_buf], writes=[sm_buf])))(),
                    (lambda rs=rs, sc=sc, sm_buf=sm_buf: (lambda: P.op(
                        "vector", lambda h: h.reciprocal(rs, sc),
                        reads=[sm_buf], writes=[sm_buf])))(),
                ]

            def quant_scales(sm_buf, sm_ap, G):
                for th in scale_bits_thunks(sm_buf, sm_ap):
                    th()

            def quant_rounds(src_buf, src_ap, dst_buf, dst_ap, sm_buf, sm_ap, G):
                W = 32 * G
                for w0 in range(0, W, 512):
                    ww = min(512, W - w0)
                    gg0 = w0 // 32
                    ggn = ww // 32
                    s3 = src_ap[:, w0:w0 + ww].rearrange("p (g e) -> p g e", e=32)
                    d3 = dst_ap[:, w0:w0 + ww].rearrange("p (g e) -> p g e", e=32)
                    rsb = sm_ap[:, 3, gg0:gg0 + ggn].to_broadcast([128, ggn, 32])
                    scb = sm_ap[:, 2, gg0:gg0 + ggn].to_broadcast([128, ggn, 32])
                    P.op("gpsimd", (lambda s3=s3, rsb=rsb, ww=ww: (lambda h: h.tensor_tensor(
                            zq[:, 0:ww].rearrange("p (g e) -> p g e", e=32), s3, rsb, ALU.mult)))(),
                         reads=[src_buf, sm_buf], writes=[b_zq])
                    P.op("vector", (lambda ww=ww: (lambda h: h.tensor_copy(mq[:, 0:ww], zq[:, 0:ww])))(),
                         reads=[b_zq], writes=[b_mq])
                    P.op("vector", (lambda d3=d3, scb=scb, ww=ww: (lambda h: h.tensor_tensor(
                            d3, mq[:, 0:ww].rearrange("p (g e) -> p g e", e=32), scb, ALU.mult)))(),
                         reads=[b_mq, sm_buf], writes=[dst_buf])

            def wfc_s1(i):
                sl = i % 2
                P.dma("gpsimd", "wfq%d" % sl, wqf[:, sl, :], wfc_h[i * 128:(i + 1) * 128, :],
                      writes=[b_wqf[sl]])

            def wfc_s2(i):
                sl = i % 2
                quant_amax(b_wqf[sl], wqf[:, sl, :], b_qs1[sl], qs1[:, sl, :, :], 32)
                quant_scales(b_qs1[sl], qs1[:, sl, :, :], 32)
                quant_rounds(b_wqf[sl], wqf[:, sl, :], b_wqb, wqb[:], b_qs1[sl], qs1[:, sl, :, :], 32)

            def wfc_s3(i):
                for j in range(8):
                    P.op("tensor",
                         (lambda j=j: (lambda h: h.matmul(
                             wqt_ps[:, j * 128:(j + 1) * 128], wqb[:, j * 128:(j + 1) * 128],
                             ident_sb[:], start=(j == 0), stop=(j == 7), is_transpose=True)))(),
                         reads=[b_wqb, b_ident],
                         wfirst=[b_wqt] if j == 0 else [], writes=[b_wqt], wlast=(j == 7))

            def wfc_s4(i):
                sl = i % 2
                P.op("vector", lambda h: h.tensor_copy(
                        wstg[:], wqt_ps[:].rearrange("p (c q) -> p c q", q=128)),
                     reads=[b_wqt], writes=[b_wstg])
                P.dma("sync", "wsp%d" % sl,
                      wfcT_h[:, :, i * 128:(i + 1) * 128].rearrange("f p q -> p f q"),
                      wstg[:], reads=[b_wstg], writes=[wfcT_bufs[i]])

            def wpj_s1(c, half):
                sl = half
                P.dma("gpsimd", "wpq%d" % sl, wpf[:, sl, :],
                      wpj_h[c * 128:(c + 1) * 128, half * 2048:(half + 1) * 2048],
                      writes=[b_wpf[sl]])

            def wpj_s2(c, half):
                sl = half
                quant_amax(b_wpf[sl], wpf[:, sl, :], b_qs2[sl], qs2[:, sl, :, :], 64)
                quant_scales(b_qs2[sl], qs2[:, sl, :, :], 64)
                quant_rounds(b_wpf[sl], wpf[:, sl, :], b_wpb, wpb[:], b_qs2[sl], qs2[:, sl, :, :], 64)

            def wpj_s3(c, half):
                for g2 in range(2):
                    if g2 == 1:
                        grp = half * 2
                        P.op("scalar",
                             (lambda grp=grp, c=c: (lambda h: h.activation(
                                 wpjT[:, grp * 8:(grp + 1) * 8, c * 128:(c + 1) * 128],
                                 wqt_ps[:].rearrange("p (c q) -> p c q", q=128), AF.Identity)))(),
                             reads=[b_wqt], writes=[b_wpjT])
                    for j in range(8):
                        hl = g2 * 8 + j
                        P.op("tensor",
                             (lambda hl=hl, j=j: (lambda h: h.matmul(
                                 wqt_ps[:, j * 128:(j + 1) * 128], wpb[:, hl * 128:(hl + 1) * 128],
                                 ident_sb[:], start=(j == 0), stop=(j == 7), is_transpose=True)))(),
                             reads=[b_wpb, b_ident],
                             wfirst=[b_wqt] if j == 0 else [], writes=[b_wqt], wlast=(j == 7))

            def wpj_s4(c, half):
                grp = half * 2 + 1
                P.op("vector",
                     (lambda grp=grp, c=c: (lambda h: h.tensor_copy(
                         wpjT[:, grp * 8:(grp + 1) * 8, c * 128:(c + 1) * 128],
                         wqt_ps[:].rearrange("p (c q) -> p c q", q=128))))(),
                     reads=[b_wqt], writes=[b_wpjT])

            wq_jobs = []
            for k in range(16):
                wq_jobs.append(("fc", 2 * k))
                wq_jobs.append(("fc", 2 * k + 1))
                if k % 2 == 0:
                    c = k // 2
                    wq_jobs.append(("pj", (c, 0)))
                    wq_jobs.append(("pj", (c, 1)))
            wq_q = [None, None, None, None]

            def wq_front():
                if wq_q[3] is not None:
                    kind, arg = wq_q[3]
                    (wfc_s4 if kind == "fc" else (lambda a: wpj_s4(*a)))(arg)
                    wq_q[3] = None

            def wq_step():
                wq_front()
                if wq_q[2] is not None:
                    kind, arg = wq_q[2]
                    (wfc_s3 if kind == "fc" else (lambda a: wpj_s3(*a)))(arg)
                wq_q[3] = wq_q[2]
                wq_q[2] = wq_q[1]
                if wq_q[1] is not None:
                    kind, arg = wq_q[1]
                    (wfc_s2 if kind == "fc" else (lambda a: wpj_s2(*a)))(arg)
                wq_q[1] = None
                if wq_jobs:
                    kind, arg = wq_jobs.pop(0)
                    (wfc_s1 if kind == "fc" else (lambda a: wpj_s1(*a)))(arg)
                    wq_q[1] = (kind, arg)

            P.dma("sync", "lld0", latB[:, 0, :, :],
                  lat_h[:, :, 0:QB].rearrange("c p q -> p c q"),
                  extra=[lat_bufs[0].w], writes=[b_latB[0]])
            for qb in range(NQB):
                lsl = 0

                qslots = (se_ps[:, 0:512], se_ps[:, 512:1024], so_ps[:, 0:512], so_ps[:, 512:1024])
                qsbufs = (b_s[0][0], b_s[0][1], b_s[1][0], b_s[1][1])
                for oc in range(8):
                    ps = qslots[oc % 4]
                    pb = qsbufs[oc % 4]
                    for l in range(4):
                        P.op("tensor",
                             (lambda oc=oc, l=l, qb=qb, ps=ps, lsl=lsl: (lambda h: h.matmul(
                                 ps, wqu_sb[:, l, oc * 128:(oc + 1) * 128],
                                 latB[:, lsl, l, :],
                                 start=(l == 0), stop=(l == 3))))(),
                             reads=[b_wqu, b_latB[lsl]],
                             wfirst=[pb] if l == 0 else [], writes=[pb], wlast=(l == 3))
                    P.op("scalar",
                         (lambda oc=oc, ps=ps: (lambda h: h.activation(q_sb[:, oc, :], ps, AF.Identity)))(),
                         reads=[pb], writes=[b_q[oc]])

                if qb + 1 < NQB:
                    P.dma("sync", "lld%d" % ((qb + 1) % 2), latB[:, 0, :, :],
                          lat_h[:, :, (qb + 1) * QB:(qb + 2) * QB].rearrange("c p q -> p c q"),
                          extra=[lat_bufs[qb + 1].w], writes=[b_latB[0]])

                nkt = 4 * qb + 4
                npair = nkt // 2

                def emit_qk4(oc, p):
                    for par in range(2):
                        base = 64 * par
                        sps = (se_ps, so_ps)[par]
                        for j in range(2):
                            spb = b_s[par][j]
                            kt = 2 * p + j
                            cc = max(0, 128 * kt - 512 * qb)
                            P.op("tensor",
                                 (lambda sps=sps, base=base, kt=kt, oc=oc, cc=cc, j=j: (lambda h: h.matmul(
                                     sps[:, j * 512 + cc:(j + 1) * 512],
                                     kf[base:base + 64, kt * 128:(kt + 1) * 128],
                                     q_sb[base:base + 64, oc, cc:512], start=True, stop=True)))(),
                                 reads=[b_kf2 if par else b_kf, b_q[oc]],
                                 wfirst=[spb], writes=[spb], wlast=True)

                def emit_exp2(oc, p):
                    cc0 = max(0, 128 * (2 * p) - 512 * qb)
                    sl = p % 2
                    for par in range(2):
                        sps = (se_ps, so_ps)[par]
                        ptb = b_pT[par][sl]
                        P.op("scalar",
                             (lambda par=par, sps=sps, cc0=cc0, sl=sl: (lambda h: h.activation(
                                 pT[:, par, sl, cc0:1024], sps[:, cc0:1024], AF.Exp, scale=0.125)))(),
                             reads=[b_s[par][0], b_s[par][1]], writes=[ptb])
                        for j in range(2):
                            kt = 2 * p + j
                            if kt >= 4 * qb:
                                cc = 128 * kt - 512 * qb
                                col = j * 512 + cc
                                P.op("gpsimd",
                                     (lambda par=par, col=col, sl=sl: (lambda h: h.tensor_tensor(
                                         pT[:, par, sl, col:col + 128], pT[:, par, sl, col:col + 128],
                                         tri_sb[:], ALU.mult)))(),
                                     reads=[b_tri], writes=[ptb])

                def emit_pv4(oc, p):
                    sl = p % 2
                    for par in range(2):
                        yps = (y0, y1)[par]
                        ypb = b_y_ps[par]
                        ptb = b_pT[par][sl]
                        for j in range(2):
                            kt = 2 * p + j
                            cc = max(0, 128 * kt - 512 * qb)
                            P.op("tensor",
                                 (lambda yps=yps, par=par, kt=kt, cc=cc, j=j, nkt=nkt, sl=sl: (lambda h: h.matmul(
                                     yps[0:65, cc:512], v_tm[:, kt, 0:65],
                                     pT[:, par, sl, j * 512 + cc:(j + 1) * 512],
                                     start=(kt == 0), stop=(kt == nkt - 1))))(),
                                 reads=[b_vtm[kt], ptb],
                                 wfirst=[ypb] if kt == 0 else [], writes=[ypb], wlast=(kt == nkt - 1))

                def emit_ycopy(oc):
                    for par in range(2):
                        yps = (y0, y1)[par]
                        ypb = b_y_ps[par]
                        row = 32 * ((2 * oc + par) % 4)
                        P.op("vector",
                             (lambda yps=yps, row=row: (lambda h: h.tensor_copy(
                                 dens_sb[row:row + 1, :], yps[64:65, :])))(),
                             reads=[ypb], writes=[b_dens])
                        P.op("vector",
                             (lambda yps=yps, oc=oc, par=par: (lambda h: h.tensor_copy(
                                 y_sb[64 * par:64 * par + 64, oc, :], yps[0:64, :])))(),
                             reads=[ypb], writes=[b_y[oc]])

                def emit_div_recip():
                    def _recip_lp(h):
                        with nc.allow_low_precision(reason="softmax reciprocal consumed as bf16 matmul operand"):
                            return h.reciprocal(recba_sb[:], dens_sb[:])
                    P.op("vector", _recip_lp, reads=[b_dens], writes=[b_recba])

                def emit_div_head(head):
                    oc2, par = head // 2, head % 2
                    m = head % 4
                    P.op("tensor",
                         (lambda m=m: (lambda h: h.matmul(
                             bc_ps[0:64, :], sel_sb[:, m * 64:(m + 1) * 64],
                             recba_sb[:], start=True, stop=True)))(),
                         reads=[b_sel, b_recba], wfirst=[b_bc], writes=[b_bc])
                    P.op("scalar",
                         (lambda par=par: (lambda h: h.activation(
                             bcs_sb[64 * par:64 * par + 64, :], bc_ps[0:64, :], AF.Identity)))(),
                         reads=[b_bc], writes=[b_bcs])
                    P.op("gpsimd",
                         (lambda oc2=oc2, par=par: (lambda h: h.tensor_tensor(
                             y_sb[64 * par:64 * par + 64, oc2, :],
                             y_sb[64 * par:64 * par + 64, oc2, :],
                             bcs_sb[64 * par:64 * par + 64, :], ALU.mult)))(),
                         reads=[b_bcs, b_y[oc2]], writes=[b_y[oc2]])

                wq_active = qb >= max(0, NQB - 6)
                for oc in range(8):
                    if wq_active:
                        wq_front()
                    pend_heads = list(range(2 * (oc - 2), 2 * (oc - 2) + 4)) if (oc % 2 == 0 and oc >= 2) else []
                    for p in range(npair):
                        emit_qk4(oc, p)
                        if p > 0:
                            emit_pv4(oc, p - 1)
                        emit_exp2(oc, p)
                        if p >= 1 and pend_heads:
                            emit_div_head(pend_heads.pop(0))
                    emit_pv4(oc, npair - 1)
                    while pend_heads:
                        emit_div_head(pend_heads.pop(0))
                    emit_ycopy(oc)
                    if oc % 2 == 1 and oc < 7:
                        emit_div_recip()
                    if wq_active:
                        wq_step()
                emit_div_recip()
                for head in range(12, 16):
                    emit_div_head(head)
                if qb == NQB - 1:
                    while wq_jobs or any(x is not None for x in wq_q[1:]):
                        wq_step()

                for tt in range(4):
                    ti = qb * 4 + tt
                    sl = tt % 2
                    P.dma("scalar", "xb%d" % sl, xA[:, sl, :], x_h[ti * 128:(ti + 1) * 128, :],
                          writes=[b_xA[sl]])
                    for half in range(2):
                        ps = (se_ps[:, 0:512], so_ps[:, 0:512])[half]
                        pb = (b_se, b_so)[half]
                        for oc in range(8):
                            P.op("tensor",
                                 (lambda ps=ps, oc=oc, tt=tt, half=half: (lambda h: h.matmul(
                                     ps, y_sb[:, oc, tt * 128:(tt + 1) * 128],
                                     wcp_sb[:, oc, half * 512:(half + 1) * 512],
                                     start=(oc == 0), stop=(oc == 7))))(),
                                 reads=[b_y[oc], b_wcp],
                                 wfirst=[pb] if oc == 0 else [], writes=[pb], wlast=(oc == 7))
                        P.op("vector",
                             (lambda ps=ps, sl=sl, half=half: (lambda h: h.tensor_tensor(
                                 xA[:, sl, half * 512:(half + 1) * 512], ps,
                                 xA[:, sl, half * 512:(half + 1) * 512], ALU.add)))(),
                             reads=[pb, b_xA[sl]], writes=[b_xA[sl]])
                    P.dma("scalar", "x2o%d" % sl, x2_h[ti * 128:(ti + 1) * 128, :], xA[:, sl, :],
                          reads=[b_xA[sl]], writes=[x2_bufs[ti]])

            P.barrier()
            cut = P.mark()
            _replay(nc, P, sems, {e: 0 for e in ENGS}, cut, waited_all)

        with contextlib.ExitStack() as c_stack:
            wfcT_r = c_stack.enter_context(nc.sbuf_tensor("wfcT_r", [128, 3, 8, 512], BF16))
            xC = c_stack.enter_context(nc.sbuf_tensor("xC", [128, 2, D], F32))
            ln2f = c_stack.enter_context(nc.sbuf_tensor("ln2f", [128, 2, D], F32))
            xqb = c_stack.enter_context(nc.sbuf_tensor("xqb", [128, 2, D], BF16))
            xqT = c_stack.enter_context(nc.sbuf_tensor("xqT", [128, 2, 8, 128], BF16))
            hf = c_stack.enter_context(nc.sbuf_tensor("hf", [128, 2, HID], F32))
            hqb = c_stack.enter_context(nc.sbuf_tensor("hqb", [128, 2, HID], BF16))
            hqT = c_stack.enter_context(nc.sbuf_tensor("hqT", [128, 32, 128], BF16))
            wfcT_res = c_stack.enter_context(nc.sbuf_tensor("wfcT_res", [128, 2, 8, 512], BF16))
            lnsmC = c_stack.enter_context(nc.sbuf_tensor("lnsmC", [128, 2, 8], F32))
            qsx = c_stack.enter_context(nc.sbuf_tensor("qsx", [128, 2, 4, 32], F32))
            qsh = c_stack.enter_context(nc.sbuf_tensor("qsh", [128, 2, 4, 128], F32))
            zh = c_stack.enter_context(nc.sbuf_tensor("zh", [128, 2, 512], F32))
            mh = c_stack.enter_context(nc.sbuf_tensor("mh", [128, 2, 512], I32))
            bfc_sb = c_stack.enter_context(nc.sbuf_tensor("bfc_sb", [1, HID], BF16))
            bpj_sb = c_stack.enter_context(nc.sbuf_tensor("bpj_sb", [1, D], BF16))
            psTC = c_stack.enter_context(nc.psum_tensor("psTC", [128, 1024], BF16))
            fc0 = c_stack.enter_context(nc.psum_tensor("fc0", [128, 512], F32))
            fc1 = c_stack.enter_context(nc.psum_tensor("fc1", [128, 512], F32))
            hT0 = c_stack.enter_context(nc.psum_tensor("hT0", [128, 1024], BF16))
            hT1 = c_stack.enter_context(nc.psum_tensor("hT1", [128, 1024], BF16))
            pj0 = c_stack.enter_context(nc.psum_tensor("pj0", [128, 512], F32))
            pj1 = c_stack.enter_context(nc.psum_tensor("pj1", [128, 512], F32))
            _dummy_c = 0
            b_wfcr = [Buf(), Buf(), Buf()]
            b_wfcres = [Buf(), Buf()]
            b_xC = [Buf(), Buf()]
            b_ln2 = [Buf(), Buf()]
            b_xq = [Buf(), Buf()]
            b_xqT = [Buf(), Buf()]
            b_hf = [[Buf() for _ in range(8)] for _ in range(2)]
            b_hq = [[Buf() for _ in range(8)] for _ in range(2)]
            b_hqT = [Buf()]
            b_lnC = [Buf(), Buf()]
            b_qsx = [Buf(), Buf()]
            b_qsh = [Buf(), Buf()]
            b_zh = [Buf(), Buf()]; b_mh = [Buf(), Buf()]
            b_bfc = Buf(); b_bpj = Buf()
            b_psTC = Buf()
            b_fc = [Buf(), Buf()]
            b_hT = [Buf(), Buf()]
            b_pj = [Buf(), Buf()]

            P.dma("gpsimd", "gbias", bfc_sb[:], bfc_h, writes=[b_bfc])
            ev_b = P.dma("gpsimd", "gbias", bpj_sb[:], bpj_h, writes=[b_bpj])
            b_bfc.w = ev_b

            def xquant_scales_thunks(t):
                am = qsx[:, t, 0, :]
                src3 = ln2f[:, t, :].rearrange("p (g e) -> p g e", e=32)
                return [
                    (lambda am=am, src3=src3, t=t: (lambda: P.op("vector", lambda h: h.tensor_reduce(
                            am.rearrange("p (g o) -> p g o", o=1), src3, AX.X, ALU.max,
                            apply_absolute_value=True),
                         reads=[b_ln2[t]], writes=[b_qsx[t]])))(),
                    (lambda am=am, t=t: (lambda: P.op("vector", lambda h: h.tensor_scalar_max(am, am, TINY),
                         reads=[b_qsx[t]], writes=[b_qsx[t]])))(),
                ] + scale_bits_thunks(b_qsx[t], qsx[:, t, :, :])

            def xquant_rounds(t):
                for ch in range(2):
                    sl = ch % 2
                    sc_b = qsx[:, t, 2, ch * 16:(ch + 1) * 16].to_broadcast([128, 16, 32])
                    rs_b = qsx[:, t, 3, ch * 16:(ch + 1) * 16].to_broadcast([128, 16, 32])
                    src3 = ln2f[:, t, ch * 512:(ch + 1) * 512].rearrange("p (g e) -> p g e", e=32)
                    dst3 = xqb[:, t, ch * 512:(ch + 1) * 512].rearrange("p (g e) -> p g e", e=32)
                    P.op("vector", (lambda src3=src3, rs_b=rs_b, sl=sl: (lambda h: h.tensor_tensor(
                            zh[:, sl, :].rearrange("p (g e) -> p g e", e=32), src3, rs_b, ALU.mult)))(),
                         reads=[b_ln2[t], b_qsx[t]], writes=[b_zh[sl]])
                    P.op("vector", (lambda sl=sl: (lambda h: h.tensor_copy(mh[:, sl, :], zh[:, sl, :])))(),
                         reads=[b_zh[sl]], writes=[b_mh[sl]])
                    P.op("vector", (lambda dst3=dst3, sc_b=sc_b, sl=sl: (lambda h: h.tensor_tensor(
                            dst3, mh[:, sl, :].rearrange("p (g e) -> p g e", e=32), sc_b, ALU.mult)))(),
                         reads=[b_mh[sl], b_qsx[t]], writes=[b_xq[t]])

            def xqT_emit(t):
                for j in range(8):
                    P.op("tensor",
                         (lambda t=t, j=j: (lambda h: h.matmul(
                             psTC[:, j * 128:(j + 1) * 128], xqb[:, t, j * 128:(j + 1) * 128],
                             ident_sb[:], start=(j == 0), stop=(j == 7), is_transpose=True)))(),
                         reads=[b_xq[t], b_ident],
                         wfirst=[b_psTC] if j == 0 else [], writes=[b_psTC], wlast=(j == 7))
                P.op("scalar",
                     (lambda t=t: (lambda h: h.activation(
                         xqT[:, t, :, :], psTC[:].rearrange("p (c q) -> p c q", q=128), AF.Identity)))(),
                     reads=[b_psTC], writes=[b_xqT[t]])

            def front_a_loads(st):
                for t in range(2):
                    ti = 2 * st + t
                    P.dma("scalar", "x2i%d" % t, xC[:, t, :], x2_h[ti * 128:(ti + 1) * 128, :],
                          extra=[x2_bufs[ti].w], writes=[b_xC[t]])

            def _fa_chains():
                return [layernorm_thunks(b_xC[t], xC[:, t, :], ln2f[:, t, :], b_ln2[t],
                                         lnsmC[:, t, :], b_lnC[t], ln2f[:, t, :], b_ln2[t],
                                         sq_gpsimd=True)
                        + xquant_scales_thunks(t) for t in range(2)]

            def fa_v1(st):
                interleave(*[c[0:7] for c in _fa_chains()])

            def fa_s(st):
                interleave(*[c[7:11] for c in _fa_chains()])

            def fa_v2(st):
                interleave(*[c[11:] for c in _fa_chains()])
                for t in range(2):
                    xquant_rounds(t)

            def front_a_T(st):
                emit_wld(2)
                emit_wld(3)
                emit_wld(4)
                for t in range(2):
                    xqT_emit(t)

            def emit_wld(hb):
                sl = hb % 3
                ld_extra = [b.w for b in wfcT_bufs]
                P.dma("sync", "wld%d" % sl, wfcT_r[:, sl, :, :],
                      wfcT_h[:, :, hb * 512:(hb + 1) * 512].rearrange("f p q -> p f q"),
                      extra=ld_extra, writes=[b_wfcr[sl]])

            def front_b(st):
                for hb in range(8):
                    if hb < 2:
                        wsrc, wbuf = wfcT_res[:, hb, :, :], b_wfcres[hb]
                    else:
                        wsrc, wbuf = wfcT_r[:, hb % 3, :, :], b_wfcr[hb % 3]
                    for t in range(2):
                        ps = (fc0, fc1)[t]
                        pb = b_fc[t]
                        for f in range(8):
                            P.op("tensor",
                                 (lambda ps=ps, t=t, f=f, wsrc=wsrc: (lambda h: h.matmul(
                                     ps[:], xqT[:, t, f, :], wsrc[:, f, :],
                                     start=(f == 0), stop=False)))(),
                                 reads=[b_xqT[t], wbuf],
                                 wfirst=[pb] if f == 0 else [], writes=[pb], wlast=False)
                        P.op("tensor",
                             (lambda ps=ps, hb=hb: (lambda h: h.matmul(
                                 ps[:], ones_sb[0:1, 0:128], bfc_sb[0:1, hb * 512:(hb + 1) * 512],
                                 start=False, stop=True)))(),
                             reads=[b_ones, b_bfc], writes=[pb])
                        P.op("scalar",
                             (lambda ps=ps, t=t, hb=hb: (lambda h: h.activation(
                                 hf[:, t, hb * 512:(hb + 1) * 512], ps[:], AF.Gelu)))(),
                             reads=[pb], writes=[b_hf[t][hb]])
                    if 2 <= hb < 5:
                        emit_wld(hb + 3)

            def tail_amax(st):
                for hb in range(8):
                    for t in range(2):
                        P.op("vector",
                             (lambda t=t, hb=hb: (lambda h: h.tensor_reduce(
                                 qsh[:, t, 0, hb * 16:(hb + 1) * 16].rearrange("p (g o) -> p g o", o=1),
                                 hf[:, t, hb * 512:(hb + 1) * 512].rearrange("p (g e) -> p g e", e=32),
                                 AX.X, ALU.max, apply_absolute_value=True)))(),
                             reads=[b_hf[t][hb]], writes=[b_qsh[t]])

            def stage_tail(st):
                for t in range(2):
                    am = qsh[:, t, 0, :]
                    P.op("vector", (lambda am=am: (lambda h: h.tensor_scalar_max(am, am, TINY)))(),
                         reads=[b_qsh[t]], writes=[b_qsh[t]])
                    for th in scale_bits_thunks(b_qsh[t], qsh[:, t, :, :]):
                        th()
                for hb in range(8):
                    for t in range(2):
                        sc_b = qsh[:, t, 2, hb * 16:(hb + 1) * 16].to_broadcast([128, 16, 32])
                        rs_b = qsh[:, t, 3, hb * 16:(hb + 1) * 16].to_broadcast([128, 16, 32])
                        src3 = hf[:, t, hb * 512:(hb + 1) * 512].rearrange("p (g e) -> p g e", e=32)
                        dst3 = hqb[:, t, hb * 512:(hb + 1) * 512].rearrange("p (g e) -> p g e", e=32)
                        rsl = t
                        P.op("gpsimd", (lambda src3=src3, rs_b=rs_b, rsl=rsl: (lambda h: h.tensor_tensor(
                                zh[:, rsl, :].rearrange("p (g e) -> p g e", e=32), src3, rs_b, ALU.mult)))(),
                             reads=[b_hf[t][hb], b_qsh[t]], writes=[b_zh[rsl]])
                        P.op("vector", (lambda rsl=rsl: (lambda h: h.tensor_copy(mh[:, rsl, :], zh[:, rsl, :])))(),
                             reads=[b_zh[rsl]], writes=[b_mh[rsl]])
                        P.op("vector", (lambda dst3=dst3, sc_b=sc_b, t=t, rsl=rsl: (lambda h: h.tensor_tensor(
                                dst3, mh[:, rsl, :].rearrange("p (g e) -> p g e", e=32), sc_b, ALU.mult)))(),
                             reads=[b_mh[rsl], b_qsh[t]], writes=[b_hq[t][hb]])

            def _hqT_emit(st, t):
                for grp in range(4):
                        hps = (hT0, hT1)[grp % 2]
                        hpb = b_hT[grp % 2]
                        for j in range(8):
                            hc = grp * 8 + j
                            P.op("tensor",
                                 (lambda hps=hps, t=t, hc=hc, j=j: (lambda h: h.matmul(
                                     hps[:, j * 128:(j + 1) * 128], hqb[:, t, hc * 128:(hc + 1) * 128],
                                     ident_sb[:], start=(j == 0), stop=(j == 7), is_transpose=True)))(),
                                 reads=[b_hq[t][hc // 4], b_ident],
                                 wfirst=[hpb] if j == 0 else [], writes=[hpb], wlast=(j == 7))
                        P.op("scalar",
                             (lambda hps=hps, t=t, grp=grp: (lambda h: h.activation(
                                 hqT[:, grp * 8:(grp + 1) * 8, :],
                                 hps[:].rearrange("p (c q) -> p c q", q=128), AF.Identity)))(),
                             reads=[hpb], writes=[b_hqT[0]])

            def stage_proj_T(st):
                _hqT_emit(st, 0)

            def stage_proj_rest(st):
                for t in range(2):
                    ti = 2 * st + t
                    P.dma("sync", "x2i%d" % (2 + t), xC[:, t, :], x2_h[ti * 128:(ti + 1) * 128, :],
                          extra=[x2_bufs[ti].w], writes=[b_xC[t]])
                    if t == 1:
                        _hqT_emit(st, 1)
                    for half in range(2):
                        ps = (pj0, pj1)[half]
                        pb = b_pj[half]
                        for hc in range(32):
                            P.op("tensor",
                                 (lambda ps=ps, t=t, hc=hc, half=half: (lambda h: h.matmul(
                                     ps[:], hqT[:, hc, :], wpjT[:, hc, half * 512:(half + 1) * 512],
                                     start=(hc == 0), stop=False)))(),
                                 reads=[b_hqT[0], b_wpjT],
                                 wfirst=[pb] if hc == 0 else [], writes=[pb], wlast=False)
                        P.op("tensor",
                             (lambda ps=ps, half=half: (lambda h: h.matmul(
                                 ps[:], ones_sb[0:1, 0:128], bpj_sb[0:1, half * 512:(half + 1) * 512],
                                 start=False, stop=True)))(),
                             reads=[b_ones, b_bpj], writes=[pb])
                        P.op("vector",
                             (lambda ps=ps, t=t, half=half: (lambda h: h.tensor_tensor(
                                 xC[:, t, half * 512:(half + 1) * 512], ps[:],
                                 xC[:, t, half * 512:(half + 1) * 512], ALU.add)))(),
                             reads=[pb, b_xC[t]], writes=[b_xC[t]])
                    P.dma("scalar", "oo%d" % t, out_h[ti * 128:(ti + 1) * 128, :], xC[:, t, :],
                          reads=[b_xC[t]])

            for hb in range(2):
                P.dma("sync", "wre%d" % hb, wfcT_res[:, hb, :, :],
                      wfcT_h[:, :, hb * 512:(hb + 1) * 512].rearrange("f p q -> p f q"),
                      extra=[b.w for b in wfcT_bufs], writes=[b_wfcres[hb]])
            def front_a_all(st):
                fa_v1(st)
                fa_s(st)
                fa_v2(st)

            front_a_loads(0)
            front_a_all(0)
            front_a_T(0)
            if NST > 1:
                front_a_loads(1)
                front_a_all(1)
            for st in range(NST):
                if st + 2 < NST:
                    front_a_loads(st + 2)
                    fa_v1(st + 2)
                front_b(st)
                if st + 1 < NST:
                    front_a_T(st + 1)
                tail_amax(st)
                if st > 0:
                    stage_proj_T(st - 1)
                if st + 2 < NST:
                    fa_s(st + 2)
                    fa_v2(st + 2)
                if st > 0:
                    stage_proj_rest(st - 1)
                stage_tail(st)
            stage_proj_T(NST - 1)
            stage_proj_rest(NST - 1)

            P.items["sync"].append(("op", (lambda h: h.nop()),
                                    (("oo0", P.dma_cnt["oo0"]), ("oo1", P.dma_cnt["oo1"]))))
            P.cnt["sync"] += 1

            end = P.mark()
            _replay(nc, P, sems, cut, end, waited_all)

    return nc


_NC_CACHE = {}


def _get_nc(T):
    if T not in _NC_CACHE:
        _NC_CACHE[T] = build_program(T)
    return _NC_CACHE[T]


def kernel(L0_attn_x, L0_attn_Wc_down, L0_attn_Wq_up, L0_attn_Wkv_up, L0_attn_Wc_proj,
           L0_ffn_w_fc, L0_ffn_b_fc, L0_ffn_w_proj, L0_ffn_b_proj):
    x = np.ascontiguousarray(np.asarray(L0_attn_x, dtype=np.float32))
    B, T, _ = x.shape
    nc = _get_nc(T)
    ident = np.eye(128, dtype=np.float32)
    tri = np.triu(np.ones((128, 128), dtype=np.float32))
    sel = np.zeros((128, 256), dtype=np.float32)
    for m in range(4):
        sel[32 * m, m * 64:(m + 1) * 64] = 1.0
    common = {
        "wdn": np.ascontiguousarray(L0_attn_Wc_down, dtype=np.float32),
        "wqu": np.ascontiguousarray(L0_attn_Wq_up, dtype=np.float32),
        "wkv": np.ascontiguousarray(L0_attn_Wkv_up, dtype=np.float32),
        "wcp": np.ascontiguousarray(L0_attn_Wc_proj, dtype=np.float32),
        "wfc": np.ascontiguousarray(L0_ffn_w_fc, dtype=np.float32),
        "bfc": np.ascontiguousarray(np.asarray(L0_ffn_b_fc, dtype=np.float32).reshape(1, -1)),
        "wpj": np.ascontiguousarray(L0_ffn_w_proj, dtype=np.float32),
        "bpj": np.ascontiguousarray(np.asarray(L0_ffn_b_proj, dtype=np.float32).reshape(1, -1)),
        "ident": ident,
        "tri": tri,
        "sel": sel,
    }
    in_maps = [dict(common, x=np.ascontiguousarray(x[i])) for i in range(B)]
    res = run_bass_kernel_spmd(nc, in_maps, core_ids=list(range(B)))
    return np.stack([np.asarray(r["out"], dtype=np.float32) for r in res.results], axis=0)


# version 68 (final)
# speedup vs baseline: 1.0149x; 1.0136x over previous
import contextlib
import numpy as np
import concourse.bass as bass
import concourse.mybir as mybir
from concourse.bass_utils import run_bass_kernel_spmd

F32 = mybir.dt.float32
I32 = mybir.dt.int32
BF16 = mybir.dt.bfloat16
AF = mybir.ActivationFunctionType
ALU = mybir.AluOpType
AX = mybir.AxisListType
LN2C = float(np.log(2.0))
TINY = float(2.0 ** -126)

ENGS = ("tensor", "vector", "scalar", "gpsimd", "sync")

D = 1024
NH = 16
HD = 64
LAT = 512
HID = 4096
QB = 512


class Buf:
    __slots__ = ("w", "rs")

    def __init__(self):
        self.w = None
        self.rs = []


class Prog:

    def __init__(self):
        self.items = {e: [] for e in ENGS}
        self.cnt = {e: 0 for e in ENGS}
        self.dma_cnt = {}

    @staticmethod
    def _collect(reads, wbufs, extra):
        deps = []
        for b in reads:
            if b.w is not None:
                deps.append(b.w)
        for b in wbufs:
            if b.w is not None:
                deps.append(b.w)
            deps.extend(b.rs)
        deps.extend(e for e in extra if e is not None)
        return deps

    def op(self, eng, fn, reads=(), writes=(), extra=(), wfirst=None, wlast=True):
        wbufs = list(writes) if wfirst is None else list(wfirst)
        deps = self._collect(reads, wbufs, extra) + self.take_pending(eng)
        self.items[eng].append(("op", fn, tuple(deps)))
        self.cnt[eng] += 1
        ev = (eng, self.cnt[eng])
        for b in reads:
            b.rs.append(ev)
        if wlast:
            for b in writes:
                b.w = ev
                b.rs = []
        return ev

    def dma(self, qeng, sem, out_ap, in_ap, reads=(), writes=(), extra=()):
        deps = self._collect(reads, writes, extra) + self.take_pending(qeng)
        self.dma_cnt[sem] = self.dma_cnt.get(sem, 0) + 16
        val = self.dma_cnt[sem]
        self.items[qeng].append(("dma", (out_ap, in_ap, sem), tuple(deps)))
        ev = (sem, val)
        for b in reads:
            b.rs.append(ev)
        for b in writes:
            b.w = ev
            b.rs = []
        return ev

    def barrier(self):
        evs = [(e, self.cnt[e]) for e in ENGS]
        evs += [(s, v) for s, v in self.dma_cnt.items()]
        self.pending = {e: [ev for ev in evs if ev[0] != e and ev[1] > 0] for e in ENGS}

    def take_pending(self, eng):
        p = getattr(self, "pending", None)
        if p and eng in p:
            return p.pop(eng)
        return []

    def mark(self):
        return {e: len(self.items[e]) for e in ENGS}


def _replay(nc, P, sems, lo, hi, waited_all):
    with nc.Block() as blk:
        def mk(eng):
            def body(handle):
                waited = waited_all[eng]
                for kind, a, deps in P.items[eng][lo[eng]:hi[eng]]:
                    for (sn, val) in deps:
                        if waited.get(sn, 0) >= val:
                            continue
                        handle.wait_ge(sems[sn], val)
                        waited[sn] = val
                    if kind == "op":
                        a(handle).then_inc(sems[eng], 1)
                    else:
                        out_ap, in_ap, sn = a
                        handle.dma_start(out=out_ap, in_=in_ap).then_inc(sems[sn], 16)
            return body
        blk.tensor(mk("tensor"))
        blk.vector(mk("vector"))
        blk.scalar(mk("scalar"))
        blk.gpsimd(mk("gpsimd"))
        blk.sync(mk("sync"))


def build_program(T):
    NT = T // 128
    NQB = T // QB
    NG = NT // 4
    NST = NT // 2

    nc = bass.Bass("TRN2", target_bir_lowering=False)
    P = Prog()

    x_h = nc.dram_tensor("x", [T, D], F32, kind="ExternalInput").ap()
    wdn_h = nc.dram_tensor("wdn", [D, LAT], F32, kind="ExternalInput").ap()
    wqu_h = nc.dram_tensor("wqu", [LAT, D], F32, kind="ExternalInput").ap()
    wkv_h = nc.dram_tensor("wkv", [LAT, 2 * HD], F32, kind="ExternalInput").ap()
    wcp_h = nc.dram_tensor("wcp", [D, D], F32, kind="ExternalInput").ap()
    wfc_h = nc.dram_tensor("wfc", [HID, D], F32, kind="ExternalInput").ap()
    bfc_h = nc.dram_tensor("bfc", [1, HID], F32, kind="ExternalInput").ap()
    wpj_h = nc.dram_tensor("wpj", [D, HID], F32, kind="ExternalInput").ap()
    bpj_h = nc.dram_tensor("bpj", [1, D], F32, kind="ExternalInput").ap()
    ident_h = nc.dram_tensor("ident", [128, 128], F32, kind="ExternalInput").ap()
    tri_h = nc.dram_tensor("tri", [128, 128], F32, kind="ExternalInput").ap()
    sel_h = nc.dram_tensor("sel", [128, 256], F32, kind="ExternalInput").ap()
    x2_h = nc.dram_tensor("x2s", [T, D], F32, kind="Internal").ap()
    wfcT_h = nc.dram_tensor("wfcTs", [8, 128, HID], BF16, kind="Internal").ap()
    lat_h = nc.dram_tensor("lats", [4, 128, T], BF16, kind="Internal").ap()
    out_h = nc.dram_tensor("out", [T, D], F32, kind="ExternalOutput").ap()

    x2_bufs = [Buf() for _ in range(NT)]
    wfcT_bufs = [Buf() for _ in range(32)]

    dma_sem_names = [
        "xa0", "xa1", "xb0", "xb1", "x2o0", "x2o1", "x2i0", "x2i1", "x2i2", "x2i3",
        "oo0", "oo1", "wsp0", "wsp1", "wld0", "wld1", "wld2", "kf2s",
        "gwin", "wfq0", "wfq1", "wpq0", "wpq1", "gbias", "lsp", "lld0", "lld1", "wre0", "wre1",
    ]

    with contextlib.ExitStack() as stack:
        ec = stack.enter_context
        ident_sb = ec(nc.sbuf_tensor("ident_sb", [128, 128], BF16))
        tri_sb = ec(nc.sbuf_tensor("tri_sb", [128, 128], BF16))
        ones_sb = ec(nc.sbuf_tensor("ones_sb", [1, 128], BF16))
        wpjT = ec(nc.sbuf_tensor("wpjT", [128, 32, D], BF16))
        wqt_ps = ec(nc.psum_tensor("wqt_ps", [128, 1024], BF16))
        sems = {e: ec(nc.semaphore("s_" + e)) for e in ENGS}
        for n_ in dma_sem_names:
            sems[n_] = ec(nc.semaphore(n_))

        waited_all = {e: {} for e in ENGS}
        b_ident = Buf(); b_tri = Buf(); b_ones = Buf()
        b_wpjT = Buf(); b_wqt = Buf()

        with contextlib.ExitStack() as ab_stack:
            latg = ab_stack.enter_context(nc.sbuf_tensor("latg", [128, 4, QB], BF16))
            latB = ab_stack.enter_context(nc.sbuf_tensor("latB", [128, 1, 4, QB], BF16))
            kf = ab_stack.enter_context(nc.sbuf_tensor("kf", [128, T], BF16))
            v_tm = ab_stack.enter_context(nc.sbuf_tensor("v_tm", [128, NT, 66], BF16))
            wdn_sb = ab_stack.enter_context(nc.sbuf_tensor("wdn_sb", [128, 8, LAT], BF16))
            wqu_sb = ab_stack.enter_context(nc.sbuf_tensor("wqu_sb", [128, 4, D], BF16))
            wkv_sb = ab_stack.enter_context(nc.sbuf_tensor("wkv_sb", [128, 4, 2 * HD], BF16))
            wcp_sb = ab_stack.enter_context(nc.sbuf_tensor("wcp_sb", [128, 8, D], BF16))
            xA = ab_stack.enter_context(nc.sbuf_tensor("xA", [128, 2, D], F32))
            ln1_sb = ab_stack.enter_context(nc.sbuf_tensor("ln1_sb", [128, 2, D], BF16))
            ln1T = ab_stack.enter_context(nc.sbuf_tensor("ln1T", [128, 8, QB], BF16))
            kv_sb = ab_stack.enter_context(nc.sbuf_tensor("kv_sb", [128, QB], BF16))
            q_sb = ab_stack.enter_context(nc.sbuf_tensor("q_sb", [128, 8, QB], BF16))
            pT = ab_stack.enter_context(nc.sbuf_tensor("pT", [128, 2, 2, 2 * QB], BF16))
            sel_sb = ab_stack.enter_context(nc.sbuf_tensor("sel_sb", [128, 256], BF16))
            dens_sb = ab_stack.enter_context(nc.sbuf_tensor("dens", [128, QB], F32))
            recba_sb = ab_stack.enter_context(nc.sbuf_tensor("recba", [128, QB], BF16))
            y_sb = ab_stack.enter_context(nc.sbuf_tensor("y_sb", [128, 8, QB], BF16))
            bcs_sb = ab_stack.enter_context(nc.sbuf_tensor("bcs", [128, QB], BF16))
            lnsm = ab_stack.enter_context(nc.sbuf_tensor("lnsm", [128, 2, 8], F32))
            wqf = ab_stack.enter_context(nc.sbuf_tensor("wqf", [128, 2, D], F32))
            wqb = ab_stack.enter_context(nc.sbuf_tensor("wqb", [128, D], BF16))
            wstg = ab_stack.enter_context(nc.sbuf_tensor("wstg", [128, 8, 128], BF16))
            wpf = ab_stack.enter_context(nc.sbuf_tensor("wpf", [128, 2, HID // 2], F32))
            wpb = ab_stack.enter_context(nc.sbuf_tensor("wpb", [128, HID // 2], BF16))
            qs1 = ab_stack.enter_context(nc.sbuf_tensor("qs1", [128, 2, 4, 32], F32))
            qs2 = ab_stack.enter_context(nc.sbuf_tensor("qs2", [128, 2, 4, 64], F32))
            zq = ab_stack.enter_context(nc.sbuf_tensor("zq", [128, 512], F32))
            mq = ab_stack.enter_context(nc.sbuf_tensor("mq", [128, 512], I32))
            qi1 = ab_stack.enter_context(nc.sbuf_tensor("qi1", [128, 64], I32))
            bc_ps = ab_stack.enter_context(nc.psum_tensor("bc_ps", [64, 512], F32))
            se_ps = ab_stack.enter_context(nc.psum_tensor("se_ps", [128, 1024], F32))
            so_ps = ab_stack.enter_context(nc.psum_tensor("so_ps", [128, 1024], F32))
            y0 = ab_stack.enter_context(nc.psum_tensor("y0", [128, 512], F32))
            y1 = ab_stack.enter_context(nc.psum_tensor("y1", [128, 512], F32))
            _dummy_ab = 0
            b_latg = Buf()
            b_latB = [Buf()]
            lat_bufs = [Buf() for _ in range(64)]
            b_kf = Buf(); b_kf2 = Buf()
            b_vtm = [Buf() for _ in range(NT)]
            b_wdn = Buf(); b_wqu = Buf(); b_wkv = Buf(); b_wcp = Buf()
            b_xA = [Buf(), Buf()]
            b_ln1 = [Buf(), Buf()]
            b_ln1T = Buf()
            b_kvsb = Buf()
            b_q = [Buf() for _ in range(8)]
            b_pT = [[Buf(), Buf()], [Buf(), Buf()]]
            b_sel = Buf(); b_dens = Buf(); b_recba = Buf()
            b_y = [Buf() for _ in range(8)]
            b_bcs = Buf()
            b_lnsm = [Buf(), Buf()]
            b_wqf = [Buf(), Buf()]
            b_wqb = Buf(); b_wstg = Buf()
            b_wpf = [Buf(), Buf()]
            b_wpb = Buf()
            b_qs1 = [Buf(), Buf()]; b_qs2 = [Buf(), Buf()]
            b_zq = Buf(); b_mq = Buf(); b_qi1 = Buf()
            b_bc = Buf()
            b_s = [[Buf(), Buf()], [Buf(), Buf()]]
            b_se = b_s[0][0]; b_so = b_s[1][0]
            b_y_ps = [Buf(), Buf()]

            P.dma("gpsimd", "gwin", ident_sb[:], ident_h, writes=[b_ident])
            P.dma("gpsimd", "gwin", tri_sb[:], tri_h, writes=[b_tri])
            P.dma("gpsimd", "gwin", sel_sb[:], sel_h, writes=[b_sel])
            P.dma("gpsimd", "gwin", wdn_sb[:], wdn_h.rearrange("(c p) m -> p c m", p=128), writes=[b_wdn])
            P.dma("gpsimd", "gwin", wqu_sb[:], wqu_h.rearrange("(c p) m -> p c m", p=128), writes=[b_wqu])
            P.dma("gpsimd", "gwin", wkv_sb[:], wkv_h.rearrange("(c p) m -> p c m", p=128), writes=[b_wkv])
            gw_total = P.dma("gpsimd", "gwin", wcp_sb[:], wcp_h.rearrange("(c p) m -> p c m", p=128), writes=[b_wcp])
            for b in (b_ident, b_tri, b_sel, b_wdn, b_wqu, b_wkv, b_wcp):
                b.w = gw_total

            P.op("vector", lambda h: h.memset(ones_sb[:], 1.0), writes=[b_ones])
            P.op("vector", lambda h: h.memset(dens_sb[:], 1.0), writes=[b_dens])
            ev_vones = P.op("vector", lambda h: h.memset(v_tm[:, :, 64:65], 1.0))
            for b in b_vtm:
                b.w = ev_vones

            def layernorm_thunks(xbuf, x_ap, out_ap, out_buf, sm, smb, ztmp, zbuf, ztmp_w=None,
                                 sq_gpsimd=False):
                if ztmp_w is None:
                    sq = [
                        (lambda: P.op("gpsimd", lambda h: h.tensor_tensor(ztmp, x_ap, x_ap, ALU.mult),
                                      reads=[xbuf], writes=[zbuf])) if sq_gpsimd else
                        (lambda: P.op("scalar", lambda h: h.activation(ztmp, x_ap, AF.Square),
                                      reads=[xbuf], writes=[zbuf])),
                        lambda: P.op("vector", lambda h: h.reduce_sum(sm[:, 1:2], ztmp, AX.X),
                                     reads=[zbuf], writes=[smb]),
                    ]
                else:
                    sq = [
                        lambda: P.op("scalar", lambda h: h.activation(ztmp, x_ap[:, 0:ztmp_w], AF.Square),
                                     reads=[xbuf], writes=[zbuf]),
                        lambda: P.op("vector", lambda h: h.reduce_sum(sm[:, 1:2], ztmp, AX.X),
                                     reads=[zbuf], writes=[smb]),
                        lambda: P.op("scalar", lambda h: h.activation(ztmp, x_ap[:, ztmp_w:D], AF.Square),
                                     reads=[xbuf], writes=[zbuf]),
                        lambda: P.op("vector", lambda h: h.reduce_sum(sm[:, 3:4], ztmp, AX.X),
                                     reads=[zbuf], writes=[smb]),
                        lambda: P.op("vector", lambda h: h.tensor_tensor(sm[:, 1:2], sm[:, 1:2], sm[:, 3:4], ALU.add),
                                     reads=[smb], writes=[smb]),
                    ]
                return [
                    lambda: P.op("vector", lambda h: h.reduce_sum(sm[:, 0:1], x_ap, AX.X),
                                 reads=[xbuf], writes=[smb]),
                ] + sq + [
                    lambda: P.op("vector", lambda h: h.tensor_scalar_mul(sm[:, 2:3], sm[:, 0:1], 1.0 / D),
                                 reads=[smb], writes=[smb]),
                    lambda: P.op("vector", lambda h: h.tensor_tensor(sm[:, 3:4], sm[:, 2:3], sm[:, 2:3], ALU.mult),
                                 reads=[smb], writes=[smb]),
                    lambda: P.op("vector", lambda h: h.scalar_tensor_tensor(sm[:, 4:5], sm[:, 1:2], 1.0 / D, sm[:, 3:4], ALU.mult, ALU.subtract),
                                 reads=[smb], writes=[smb]),
                    lambda: P.op("vector", lambda h: h.tensor_scalar_add(sm[:, 4:5], sm[:, 4:5], 1e-5),
                                 reads=[smb], writes=[smb]),
                    lambda: P.op("scalar", lambda h: h.activation(sm[:, 5:6], sm[:, 4:5], AF.Ln),
                                 reads=[smb], writes=[smb]),
                    lambda: P.op("scalar", lambda h: h.activation(sm[:, 6:7], sm[:, 5:6], AF.Exp, scale=-0.5),
                                 reads=[smb], writes=[smb]),
                    lambda: P.op("vector", lambda h: h.scalar_tensor_tensor(sm[:, 7:8], sm[:, 2:3], -1.0, sm[:, 6:7], ALU.mult, ALU.mult),
                                 reads=[smb], writes=[smb]),
                    lambda: P.op("scalar", lambda h: h.activation(out_ap, x_ap, AF.Identity, bias=sm[:, 7:8], scale=sm[:, 6:7]),
                                 reads=[xbuf, smb], writes=[out_buf]),
                ]

            def layernorm(xbuf, x_ap, out_ap, out_buf, sm, smb, ztmp, zbuf, ztmp_w=None):
                for th in layernorm_thunks(xbuf, x_ap, out_ap, out_buf, sm, smb, ztmp, zbuf, ztmp_w):
                    th()

            def interleave(*lists):
                n = max(len(l) for l in lists)
                for i in range(n):
                    for l in lists:
                        if i < len(l):
                            l[i]()

            def a_ln_pair(p):
                chains = []
                for sl in range(2):
                    ti = 2 * p + sl
                    P.dma("sync", "xa%d" % sl, xA[:, sl, :], x_h[ti * 128:(ti + 1) * 128, :],
                          writes=[b_xA[sl]])
                    ztmp = (zq[:], dens_sb[:])[sl]
                    zbuf = (b_zq, b_dens)[sl]
                    chains.append(layernorm_thunks(
                        b_xA[sl], xA[:, sl, :], ln1_sb[:, sl, :], b_ln1[sl],
                        lnsm[:, sl, :], b_lnsm[sl], ztmp, zbuf, ztmp_w=512))
                interleave(*chains)

            def a_trans_pair(p):
                for sl in range(2):
                    i = 2 * (p % 2) + sl
                    for j in range(8):
                        P.op("tensor",
                             (lambda sl=sl, j=j: (lambda h: h.matmul(
                                 wqt_ps[:, j * 128:(j + 1) * 128], ln1_sb[:, sl, j * 128:(j + 1) * 128],
                                 ident_sb[:], start=(j == 0), stop=(j == 7), is_transpose=True)))(),
                             reads=[b_ln1[sl], b_ident],
                             wfirst=[b_wqt] if j == 0 else [], writes=[b_wqt], wlast=(j == 7))
                    P.op("vector",
                         (lambda i=i: (lambda h: h.tensor_copy(
                             ln1T[:, :, i * 128:(i + 1) * 128],
                             wqt_ps[:].rearrange("p (c q) -> p c q", q=128))))(),
                         reads=[b_wqt], writes=[b_ln1T])

            def a_latent_kv(g):
                for lc in range(4):
                    ps = (se_ps[:, 0:512], se_ps[:, 512:1024], so_ps[:, 0:512], so_ps[:, 512:1024])[lc]
                    pb = (b_s[0][0], b_s[0][1], b_s[1][0], b_s[1][1])[lc]
                    for f in range(8):
                        P.op("tensor",
                             (lambda lc=lc, f=f, ps=ps: (lambda h: h.matmul(
                                 ps, wdn_sb[:, f, lc * 128:(lc + 1) * 128], ln1T[:, f, :],
                                 start=(f == 0), stop=(f == 7))))(),
                             reads=[b_wdn, b_ln1T],
                             wfirst=[pb] if f == 0 else [], writes=[pb], wlast=(f == 7))
                    P.op("vector",
                         (lambda lc=lc, ps=ps: (lambda h: h.tensor_copy(
                             latg[:, lc, :], ps)))(),
                         reads=[pb], writes=[b_latg])
                for l in range(4):
                    P.op("tensor",
                         (lambda l=l: (lambda h: h.matmul(
                             se_ps[:, 0:512], wkv_sb[:, l, :], latg[:, l, :],
                             start=(l == 0), stop=(l == 3))))(),
                         reads=[b_wkv, b_latg],
                         wfirst=[b_se] if l == 0 else [], writes=[b_se], wlast=(l == 3))
                P.dma("sync", "lsp", lat_h[:, :, g * QB:(g + 1) * QB].rearrange("c p q -> p c q"),
                      latg[:], reads=[b_latg], writes=[lat_bufs[g]])
                P.op("vector", lambda h: h.tensor_copy(kv_sb[:], se_ps[:, 0:512]),
                     reads=[b_se], writes=[b_kvsb])
                P.op("vector",
                     (lambda g=g: (lambda h: h.tensor_copy(kf[0:64, g * QB:(g + 1) * QB], kv_sb[0:64, :])))(),
                     reads=[b_kvsb], writes=[b_kf])
                for j in range(4):
                    P.op("tensor",
                         (lambda j=j: (lambda h: h.matmul(
                             wqt_ps[:, j * 128:(j + 1) * 128], kv_sb[:, j * 128:(j + 1) * 128],
                             ident_sb[:], start=(j == 0), stop=(j == 3), is_transpose=True)))(),
                         reads=[b_kvsb, b_ident],
                         wfirst=[b_wqt] if j == 0 else [], writes=[b_wqt], wlast=(j == 3))
                for j in range(4):
                    ti = 4 * g + j
                    P.op("vector",
                         (lambda j=j, ti=ti: (lambda h: h.tensor_copy(
                             v_tm[:, ti, 0:64], wqt_ps[:, j * 128 + 64:(j + 1) * 128])))(),
                         reads=[b_wqt], writes=[b_vtm[ti]])

            a_ln_pair(0)
            for p in range(2 * NG):
                a_trans_pair(p)
                if p + 1 < 2 * NG:
                    a_ln_pair(p + 1)
                if p % 2 == 1:
                    a_latent_kv(p // 2)
            P.dma("sync", "kf2s", kf[64:128, :], kf[0:64, :], reads=[b_kf], writes=[b_kf2])
            P.op("vector", lambda h: h.memset(dens_sb[:], 1.0), writes=[b_dens])

            def quant_amax(src_buf, src_ap, sm_buf, sm_ap, G):
                src3 = src_ap.rearrange("p (g e) -> p g e", e=32)
                am = sm_ap[:, 0, :]
                P.op("vector", (lambda am=am, src3=src3: (lambda h: h.tensor_reduce(
                        am.rearrange("p (g o) -> p g o", o=1), src3, AX.X, ALU.max,
                        apply_absolute_value=True)))(),
                     reads=[src_buf], writes=[sm_buf])
                P.op("gpsimd", (lambda am=am: (lambda h: h.tensor_scalar_max(am, am, TINY)))(),
                     reads=[sm_buf], writes=[sm_buf])

            def scale_bits_thunks(sm_buf, sm_ap):
                am_i = sm_ap[:, 0, :].bitcast(I32)
                sc = sm_ap[:, 2, :]; rs = sm_ap[:, 3, :]
                sc_i = sc.bitcast(I32)
                return [
                    (lambda sc_i=sc_i, am_i=am_i, sm_buf=sm_buf: (lambda: P.op(
                        "vector", lambda h: h.tensor_scalar(
                            sc_i, am_i, 0x7F800000, None, ALU.bitwise_and),
                        reads=[sm_buf], writes=[sm_buf])))(),
                    (lambda sc=sc, sm_buf=sm_buf: (lambda: P.op(
                        "vector", lambda h: h.tensor_scalar_mul(sc, sc, 0.125),
                        reads=[sm_buf], writes=[sm_buf])))(),
                    (lambda rs=rs, sc=sc, sm_buf=sm_buf: (lambda: P.op(
                        "vector", lambda h: h.reciprocal(rs, sc),
                        reads=[sm_buf], writes=[sm_buf])))(),
                ]

            def quant_scales(sm_buf, sm_ap, G):
                for th in scale_bits_thunks(sm_buf, sm_ap):
                    th()

            def quant_rounds(src_buf, src_ap, dst_buf, dst_ap, sm_buf, sm_ap, G):
                W = 32 * G
                for w0 in range(0, W, 512):
                    ww = min(512, W - w0)
                    gg0 = w0 // 32
                    ggn = ww // 32
                    s3 = src_ap[:, w0:w0 + ww].rearrange("p (g e) -> p g e", e=32)
                    d3 = dst_ap[:, w0:w0 + ww].rearrange("p (g e) -> p g e", e=32)
                    rsb = sm_ap[:, 3, gg0:gg0 + ggn].to_broadcast([128, ggn, 32])
                    scb = sm_ap[:, 2, gg0:gg0 + ggn].to_broadcast([128, ggn, 32])
                    P.op("gpsimd", (lambda s3=s3, rsb=rsb, ww=ww: (lambda h: h.tensor_tensor(
                            zq[:, 0:ww].rearrange("p (g e) -> p g e", e=32), s3, rsb, ALU.mult)))(),
                         reads=[src_buf, sm_buf], writes=[b_zq])
                    P.op("vector", (lambda ww=ww: (lambda h: h.tensor_copy(mq[:, 0:ww], zq[:, 0:ww])))(),
                         reads=[b_zq], writes=[b_mq])
                    P.op("vector", (lambda d3=d3, scb=scb, ww=ww: (lambda h: h.tensor_tensor(
                            d3, mq[:, 0:ww].rearrange("p (g e) -> p g e", e=32), scb, ALU.mult)))(),
                         reads=[b_mq, sm_buf], writes=[dst_buf])

            def wfc_s1(i):
                sl = i % 2
                P.dma("gpsimd", "wfq%d" % sl, wqf[:, sl, :], wfc_h[i * 128:(i + 1) * 128, :],
                      writes=[b_wqf[sl]])

            def wfc_s2(i):
                sl = i % 2
                quant_amax(b_wqf[sl], wqf[:, sl, :], b_qs1[sl], qs1[:, sl, :, :], 32)
                quant_scales(b_qs1[sl], qs1[:, sl, :, :], 32)
                quant_rounds(b_wqf[sl], wqf[:, sl, :], b_wqb, wqb[:], b_qs1[sl], qs1[:, sl, :, :], 32)

            def wfc_s3(i):
                for j in range(8):
                    P.op("tensor",
                         (lambda j=j: (lambda h: h.matmul(
                             wqt_ps[:, j * 128:(j + 1) * 128], wqb[:, j * 128:(j + 1) * 128],
                             ident_sb[:], start=(j == 0), stop=(j == 7), is_transpose=True)))(),
                         reads=[b_wqb, b_ident],
                         wfirst=[b_wqt] if j == 0 else [], writes=[b_wqt], wlast=(j == 7))

            def wfc_s4(i):
                sl = i % 2
                P.op("vector", lambda h: h.tensor_copy(
                        wstg[:], wqt_ps[:].rearrange("p (c q) -> p c q", q=128)),
                     reads=[b_wqt], writes=[b_wstg])
                P.dma("sync", "wsp%d" % sl,
                      wfcT_h[:, :, i * 128:(i + 1) * 128].rearrange("f p q -> p f q"),
                      wstg[:], reads=[b_wstg], writes=[wfcT_bufs[i]])

            def wpj_s1(c, half):
                sl = half
                P.dma("gpsimd", "wpq%d" % sl, wpf[:, sl, :],
                      wpj_h[c * 128:(c + 1) * 128, half * 2048:(half + 1) * 2048],
                      writes=[b_wpf[sl]])

            def wpj_s2(c, half):
                sl = half
                quant_amax(b_wpf[sl], wpf[:, sl, :], b_qs2[sl], qs2[:, sl, :, :], 64)
                quant_scales(b_qs2[sl], qs2[:, sl, :, :], 64)
                quant_rounds(b_wpf[sl], wpf[:, sl, :], b_wpb, wpb[:], b_qs2[sl], qs2[:, sl, :, :], 64)

            def wpj_s3(c, half):
                for g2 in range(2):
                    if g2 == 1:
                        grp = half * 2
                        P.op("scalar",
                             (lambda grp=grp, c=c: (lambda h: h.activation(
                                 wpjT[:, grp * 8:(grp + 1) * 8, c * 128:(c + 1) * 128],
                                 wqt_ps[:].rearrange("p (c q) -> p c q", q=128), AF.Identity)))(),
                             reads=[b_wqt], writes=[b_wpjT])
                    for j in range(8):
                        hl = g2 * 8 + j
                        P.op("tensor",
                             (lambda hl=hl, j=j: (lambda h: h.matmul(
                                 wqt_ps[:, j * 128:(j + 1) * 128], wpb[:, hl * 128:(hl + 1) * 128],
                                 ident_sb[:], start=(j == 0), stop=(j == 7), is_transpose=True)))(),
                             reads=[b_wpb, b_ident],
                             wfirst=[b_wqt] if j == 0 else [], writes=[b_wqt], wlast=(j == 7))

            def wpj_s4(c, half):
                grp = half * 2 + 1
                P.op("vector",
                     (lambda grp=grp, c=c: (lambda h: h.tensor_copy(
                         wpjT[:, grp * 8:(grp + 1) * 8, c * 128:(c + 1) * 128],
                         wqt_ps[:].rearrange("p (c q) -> p c q", q=128))))(),
                     reads=[b_wqt], writes=[b_wpjT])

            wq_jobs = []
            for k in range(16):
                wq_jobs.append(("fc", 2 * k))
                wq_jobs.append(("fc", 2 * k + 1))
                if k % 2 == 0:
                    c = k // 2
                    wq_jobs.append(("pj", (c, 0)))
                    wq_jobs.append(("pj", (c, 1)))
            wq_q = [None, None, None, None]

            def wq_front():
                if wq_q[3] is not None:
                    kind, arg = wq_q[3]
                    (wfc_s4 if kind == "fc" else (lambda a: wpj_s4(*a)))(arg)
                    wq_q[3] = None

            def wq_step():
                wq_front()
                if wq_q[2] is not None:
                    kind, arg = wq_q[2]
                    (wfc_s3 if kind == "fc" else (lambda a: wpj_s3(*a)))(arg)
                wq_q[3] = wq_q[2]
                wq_q[2] = wq_q[1]
                if wq_q[1] is not None:
                    kind, arg = wq_q[1]
                    (wfc_s2 if kind == "fc" else (lambda a: wpj_s2(*a)))(arg)
                wq_q[1] = None
                if wq_jobs:
                    kind, arg = wq_jobs.pop(0)
                    (wfc_s1 if kind == "fc" else (lambda a: wpj_s1(*a)))(arg)
                    wq_q[1] = (kind, arg)

            P.dma("sync", "lld0", latB[:, 0, :, :],
                  lat_h[:, :, 0:QB].rearrange("c p q -> p c q"),
                  extra=[lat_bufs[0].w], writes=[b_latB[0]])
            for qb in range(NQB):
                lsl = 0

                qslots = (se_ps[:, 0:512], se_ps[:, 512:1024], so_ps[:, 0:512], so_ps[:, 512:1024])
                qsbufs = (b_s[0][0], b_s[0][1], b_s[1][0], b_s[1][1])
                for oc in range(8):
                    ps = qslots[oc % 4]
                    pb = qsbufs[oc % 4]
                    for l in range(4):
                        P.op("tensor",
                             (lambda oc=oc, l=l, qb=qb, ps=ps, lsl=lsl: (lambda h: h.matmul(
                                 ps, wqu_sb[:, l, oc * 128:(oc + 1) * 128],
                                 latB[:, lsl, l, :],
                                 start=(l == 0), stop=(l == 3))))(),
                             reads=[b_wqu, b_latB[lsl]],
                             wfirst=[pb] if l == 0 else [], writes=[pb], wlast=(l == 3))
                    P.op("scalar",
                         (lambda oc=oc, ps=ps: (lambda h: h.activation(q_sb[:, oc, :], ps, AF.Identity)))(),
                         reads=[pb], writes=[b_q[oc]])

                if qb + 1 < NQB:
                    P.dma("sync", "lld%d" % ((qb + 1) % 2), latB[:, 0, :, :],
                          lat_h[:, :, (qb + 1) * QB:(qb + 2) * QB].rearrange("c p q -> p c q"),
                          extra=[lat_bufs[qb + 1].w], writes=[b_latB[0]])

                nkt = 4 * qb + 4
                npair = nkt // 2

                def emit_qk4(oc, p):
                    for par in range(2):
                        base = 64 * par
                        sps = (se_ps, so_ps)[par]
                        for j in range(2):
                            spb = b_s[par][j]
                            kt = 2 * p + j
                            cc = max(0, 128 * kt - 512 * qb)
                            P.op("tensor",
                                 (lambda sps=sps, base=base, kt=kt, oc=oc, cc=cc, j=j: (lambda h: h.matmul(
                                     sps[:, j * 512 + cc:(j + 1) * 512],
                                     kf[base:base + 64, kt * 128:(kt + 1) * 128],
                                     q_sb[base:base + 64, oc, cc:512], start=True, stop=True)))(),
                                 reads=[b_kf2 if par else b_kf, b_q[oc]],
                                 wfirst=[spb], writes=[spb], wlast=True)

                def emit_exp2(oc, p):
                    cc0 = max(0, 128 * (2 * p) - 512 * qb)
                    sl = p % 2
                    for par in range(2):
                        sps = (se_ps, so_ps)[par]
                        ptb = b_pT[par][sl]
                        P.op("scalar",
                             (lambda par=par, sps=sps, cc0=cc0, sl=sl: (lambda h: h.activation(
                                 pT[:, par, sl, cc0:1024], sps[:, cc0:1024], AF.Exp, scale=0.125)))(),
                             reads=[b_s[par][0], b_s[par][1]], writes=[ptb])
                        for j in range(2):
                            kt = 2 * p + j
                            if kt >= 4 * qb:
                                cc = 128 * kt - 512 * qb
                                col = j * 512 + cc
                                P.op("vector",
                                     (lambda par=par, col=col, sl=sl: (lambda h: h.tensor_tensor(
                                         pT[:, par, sl, col:col + 128], pT[:, par, sl, col:col + 128],
                                         tri_sb[:], ALU.mult)))(),
                                     reads=[b_tri], writes=[ptb])

                def emit_pv4(oc, p):
                    sl = p % 2
                    for par in range(2):
                        yps = (y0, y1)[par]
                        ypb = b_y_ps[par]
                        ptb = b_pT[par][sl]
                        for j in range(2):
                            kt = 2 * p + j
                            cc = max(0, 128 * kt - 512 * qb)
                            P.op("tensor",
                                 (lambda yps=yps, par=par, kt=kt, cc=cc, j=j, nkt=nkt, sl=sl: (lambda h: h.matmul(
                                     yps[0:65, cc:512], v_tm[:, kt, 0:65],
                                     pT[:, par, sl, j * 512 + cc:(j + 1) * 512],
                                     start=(kt == 0), stop=(kt == nkt - 1))))(),
                                 reads=[b_vtm[kt], ptb],
                                 wfirst=[ypb] if kt == 0 else [], writes=[ypb], wlast=(kt == nkt - 1))

                def emit_ycopy(oc):
                    for par in range(2):
                        yps = (y0, y1)[par]
                        ypb = b_y_ps[par]
                        row = 32 * ((2 * oc + par) % 4)
                        P.op("vector",
                             (lambda yps=yps, row=row: (lambda h: h.tensor_copy(
                                 dens_sb[row:row + 1, :], yps[64:65, :])))(),
                             reads=[ypb], writes=[b_dens])
                        P.op("vector",
                             (lambda yps=yps, oc=oc, par=par: (lambda h: h.tensor_copy(
                                 y_sb[64 * par:64 * par + 64, oc, :], yps[0:64, :])))(),
                             reads=[ypb], writes=[b_y[oc]])

                def emit_div_recip():
                    def _recip_lp(h):
                        with nc.allow_low_precision(reason="softmax reciprocal consumed as bf16 matmul operand"):
                            return h.reciprocal(recba_sb[:], dens_sb[:])
                    P.op("vector", _recip_lp, reads=[b_dens], writes=[b_recba])

                def emit_div_head(head):
                    oc2, par = head // 2, head % 2
                    m = head % 4
                    P.op("tensor",
                         (lambda m=m: (lambda h: h.matmul(
                             bc_ps[0:64, :], sel_sb[:, m * 64:(m + 1) * 64],
                             recba_sb[:], start=True, stop=True)))(),
                         reads=[b_sel, b_recba], wfirst=[b_bc], writes=[b_bc])
                    P.op("scalar",
                         (lambda par=par: (lambda h: h.activation(
                             bcs_sb[64 * par:64 * par + 64, :], bc_ps[0:64, :], AF.Identity)))(),
                         reads=[b_bc], writes=[b_bcs])
                    P.op("vector",
                         (lambda oc2=oc2, par=par: (lambda h: h.tensor_tensor(
                             y_sb[64 * par:64 * par + 64, oc2, :],
                             y_sb[64 * par:64 * par + 64, oc2, :],
                             bcs_sb[64 * par:64 * par + 64, :], ALU.mult)))(),
                         reads=[b_bcs, b_y[oc2]], writes=[b_y[oc2]])

                wq_active = qb >= max(0, NQB - 6)
                for oc in range(8):
                    if wq_active:
                        wq_front()
                    pend_heads = list(range(2 * (oc - 2), 2 * (oc - 2) + 4)) if (oc % 2 == 0 and oc >= 2) else []
                    for p in range(npair):
                        emit_qk4(oc, p)
                        if p > 0:
                            emit_pv4(oc, p - 1)
                        emit_exp2(oc, p)
                        if p >= 1 and pend_heads:
                            emit_div_head(pend_heads.pop(0))
                    emit_pv4(oc, npair - 1)
                    while pend_heads:
                        emit_div_head(pend_heads.pop(0))
                    emit_ycopy(oc)
                    if oc % 2 == 1 and oc < 7:
                        emit_div_recip()
                    if wq_active:
                        wq_step()
                emit_div_recip()
                for head in range(12, 16):
                    emit_div_head(head)
                if qb == NQB - 1:
                    while wq_jobs or any(x is not None for x in wq_q[1:]):
                        wq_step()

                for tt in range(4):
                    ti = qb * 4 + tt
                    sl = tt % 2
                    P.dma("scalar", "xb%d" % sl, xA[:, sl, :], x_h[ti * 128:(ti + 1) * 128, :],
                          writes=[b_xA[sl]])
                    for half in range(2):
                        ps = (se_ps[:, 0:512], so_ps[:, 0:512])[half]
                        pb = (b_se, b_so)[half]
                        for oc in range(8):
                            P.op("tensor",
                                 (lambda ps=ps, oc=oc, tt=tt, half=half: (lambda h: h.matmul(
                                     ps, y_sb[:, oc, tt * 128:(tt + 1) * 128],
                                     wcp_sb[:, oc, half * 512:(half + 1) * 512],
                                     start=(oc == 0), stop=(oc == 7))))(),
                                 reads=[b_y[oc], b_wcp],
                                 wfirst=[pb] if oc == 0 else [], writes=[pb], wlast=(oc == 7))
                        P.op("vector",
                             (lambda ps=ps, sl=sl, half=half: (lambda h: h.tensor_tensor(
                                 xA[:, sl, half * 512:(half + 1) * 512], ps,
                                 xA[:, sl, half * 512:(half + 1) * 512], ALU.add)))(),
                             reads=[pb, b_xA[sl]], writes=[b_xA[sl]])
                    P.dma("scalar", "x2o%d" % sl, x2_h[ti * 128:(ti + 1) * 128, :], xA[:, sl, :],
                          reads=[b_xA[sl]], writes=[x2_bufs[ti]])

            P.barrier()
            cut = P.mark()
            _replay(nc, P, sems, {e: 0 for e in ENGS}, cut, waited_all)

        with contextlib.ExitStack() as c_stack:
            wfcT_r = c_stack.enter_context(nc.sbuf_tensor("wfcT_r", [128, 3, 8, 512], BF16))
            xC = c_stack.enter_context(nc.sbuf_tensor("xC", [128, 2, D], F32))
            ln2f = c_stack.enter_context(nc.sbuf_tensor("ln2f", [128, 2, D], F32))
            xqb = c_stack.enter_context(nc.sbuf_tensor("xqb", [128, 2, D], BF16))
            xqT = c_stack.enter_context(nc.sbuf_tensor("xqT", [128, 2, 8, 128], BF16))
            hf = c_stack.enter_context(nc.sbuf_tensor("hf", [128, 2, HID], F32))
            hqb = c_stack.enter_context(nc.sbuf_tensor("hqb", [128, 2, HID], BF16))
            hqT = c_stack.enter_context(nc.sbuf_tensor("hqT", [128, 32, 128], BF16))
            wfcT_res = c_stack.enter_context(nc.sbuf_tensor("wfcT_res", [128, 2, 8, 512], BF16))
            lnsmC = c_stack.enter_context(nc.sbuf_tensor("lnsmC", [128, 2, 8], F32))
            qsx = c_stack.enter_context(nc.sbuf_tensor("qsx", [128, 2, 4, 32], F32))
            qsh = c_stack.enter_context(nc.sbuf_tensor("qsh", [128, 2, 4, 128], F32))
            zh = c_stack.enter_context(nc.sbuf_tensor("zh", [128, 2, 512], F32))
            mh = c_stack.enter_context(nc.sbuf_tensor("mh", [128, 2, 512], I32))
            bfc_sb = c_stack.enter_context(nc.sbuf_tensor("bfc_sb", [1, HID], BF16))
            bpj_sb = c_stack.enter_context(nc.sbuf_tensor("bpj_sb", [1, D], BF16))
            psTC = c_stack.enter_context(nc.psum_tensor("psTC", [128, 1024], BF16))
            fc0 = c_stack.enter_context(nc.psum_tensor("fc0", [128, 512], F32))
            fc1 = c_stack.enter_context(nc.psum_tensor("fc1", [128, 512], F32))
            hT0 = c_stack.enter_context(nc.psum_tensor("hT0", [128, 1024], BF16))
            hT1 = c_stack.enter_context(nc.psum_tensor("hT1", [128, 1024], BF16))
            pj0 = c_stack.enter_context(nc.psum_tensor("pj0", [128, 512], F32))
            pj1 = c_stack.enter_context(nc.psum_tensor("pj1", [128, 512], F32))
            _dummy_c = 0
            b_wfcr = [Buf(), Buf(), Buf()]
            b_wfcres = [Buf(), Buf()]
            b_xC = [Buf(), Buf()]
            b_ln2 = [Buf(), Buf()]
            b_xq = [Buf(), Buf()]
            b_xqT = [Buf(), Buf()]
            b_hf = [[Buf() for _ in range(8)] for _ in range(2)]
            b_hq = [[Buf() for _ in range(8)] for _ in range(2)]
            b_hqT = [Buf()]
            b_lnC = [Buf(), Buf()]
            b_qsx = [Buf(), Buf()]
            b_qsh = [Buf(), Buf()]
            b_zh = [Buf(), Buf()]; b_mh = [Buf(), Buf()]
            b_bfc = Buf(); b_bpj = Buf()
            b_psTC = Buf()
            b_fc = [Buf(), Buf()]
            b_hT = [Buf(), Buf()]
            b_pj = [Buf(), Buf()]

            P.dma("gpsimd", "gbias", bfc_sb[:], bfc_h, writes=[b_bfc])
            ev_b = P.dma("gpsimd", "gbias", bpj_sb[:], bpj_h, writes=[b_bpj])
            b_bfc.w = ev_b

            def xquant_scales_thunks(t):
                am = qsx[:, t, 0, :]
                src3 = ln2f[:, t, :].rearrange("p (g e) -> p g e", e=32)
                return [
                    (lambda am=am, src3=src3, t=t: (lambda: P.op("vector", lambda h: h.tensor_reduce(
                            am.rearrange("p (g o) -> p g o", o=1), src3, AX.X, ALU.max,
                            apply_absolute_value=True),
                         reads=[b_ln2[t]], writes=[b_qsx[t]])))(),
                    (lambda am=am, t=t: (lambda: P.op("vector", lambda h: h.tensor_scalar_max(am, am, TINY),
                         reads=[b_qsx[t]], writes=[b_qsx[t]])))(),
                ] + scale_bits_thunks(b_qsx[t], qsx[:, t, :, :])

            def xquant_rounds(t):
                for ch in range(2):
                    sl = ch % 2
                    sc_b = qsx[:, t, 2, ch * 16:(ch + 1) * 16].to_broadcast([128, 16, 32])
                    rs_b = qsx[:, t, 3, ch * 16:(ch + 1) * 16].to_broadcast([128, 16, 32])
                    src3 = ln2f[:, t, ch * 512:(ch + 1) * 512].rearrange("p (g e) -> p g e", e=32)
                    dst3 = xqb[:, t, ch * 512:(ch + 1) * 512].rearrange("p (g e) -> p g e", e=32)
                    P.op("vector", (lambda src3=src3, rs_b=rs_b, sl=sl: (lambda h: h.tensor_tensor(
                            zh[:, sl, :].rearrange("p (g e) -> p g e", e=32), src3, rs_b, ALU.mult)))(),
                         reads=[b_ln2[t], b_qsx[t]], writes=[b_zh[sl]])
                    P.op("vector", (lambda sl=sl: (lambda h: h.tensor_copy(mh[:, sl, :], zh[:, sl, :])))(),
                         reads=[b_zh[sl]], writes=[b_mh[sl]])
                    P.op("vector", (lambda dst3=dst3, sc_b=sc_b, sl=sl: (lambda h: h.tensor_tensor(
                            dst3, mh[:, sl, :].rearrange("p (g e) -> p g e", e=32), sc_b, ALU.mult)))(),
                         reads=[b_mh[sl], b_qsx[t]], writes=[b_xq[t]])

            def xqT_emit(t):
                for j in range(8):
                    P.op("tensor",
                         (lambda t=t, j=j: (lambda h: h.matmul(
                             psTC[:, j * 128:(j + 1) * 128], xqb[:, t, j * 128:(j + 1) * 128],
                             ident_sb[:], start=(j == 0), stop=(j == 7), is_transpose=True)))(),
                         reads=[b_xq[t], b_ident],
                         wfirst=[b_psTC] if j == 0 else [], writes=[b_psTC], wlast=(j == 7))
                P.op("scalar",
                     (lambda t=t: (lambda h: h.activation(
                         xqT[:, t, :, :], psTC[:].rearrange("p (c q) -> p c q", q=128), AF.Identity)))(),
                     reads=[b_psTC], writes=[b_xqT[t]])

            def front_a_loads(st):
                for t in range(2):
                    ti = 2 * st + t
                    P.dma("scalar", "x2i%d" % t, xC[:, t, :], x2_h[ti * 128:(ti + 1) * 128, :],
                          extra=[x2_bufs[ti].w], writes=[b_xC[t]])

            def _fa_chains():
                return [layernorm_thunks(b_xC[t], xC[:, t, :], ln2f[:, t, :], b_ln2[t],
                                         lnsmC[:, t, :], b_lnC[t], ln2f[:, t, :], b_ln2[t],
                                         sq_gpsimd=True)
                        + xquant_scales_thunks(t) for t in range(2)]

            def fa_v1(st):
                interleave(*[c[0:7] for c in _fa_chains()])

            def fa_s(st):
                interleave(*[c[7:11] for c in _fa_chains()])

            def fa_v2(st):
                interleave(*[c[11:] for c in _fa_chains()])
                for t in range(2):
                    xquant_rounds(t)

            def front_a_T(st):
                emit_wld(2)
                emit_wld(3)
                emit_wld(4)
                for t in range(2):
                    xqT_emit(t)

            def emit_wld(hb):
                sl = hb % 3
                ld_extra = [b.w for b in wfcT_bufs]
                P.dma("sync", "wld%d" % sl, wfcT_r[:, sl, :, :],
                      wfcT_h[:, :, hb * 512:(hb + 1) * 512].rearrange("f p q -> p f q"),
                      extra=ld_extra, writes=[b_wfcr[sl]])

            def front_b(st):
                for hb in range(8):
                    if hb < 2:
                        wsrc, wbuf = wfcT_res[:, hb, :, :], b_wfcres[hb]
                    else:
                        wsrc, wbuf = wfcT_r[:, hb % 3, :, :], b_wfcr[hb % 3]
                    for t in range(2):
                        ps = (fc0, fc1)[t]
                        pb = b_fc[t]
                        for f in range(8):
                            P.op("tensor",
                                 (lambda ps=ps, t=t, f=f, wsrc=wsrc: (lambda h: h.matmul(
                                     ps[:], xqT[:, t, f, :], wsrc[:, f, :],
                                     start=(f == 0), stop=False)))(),
                                 reads=[b_xqT[t], wbuf],
                                 wfirst=[pb] if f == 0 else [], writes=[pb], wlast=False)
                        P.op("tensor",
                             (lambda ps=ps, hb=hb: (lambda h: h.matmul(
                                 ps[:], ones_sb[0:1, 0:128], bfc_sb[0:1, hb * 512:(hb + 1) * 512],
                                 start=False, stop=True)))(),
                             reads=[b_ones, b_bfc], writes=[pb])
                        P.op("scalar",
                             (lambda ps=ps, t=t, hb=hb: (lambda h: h.activation(
                                 hf[:, t, hb * 512:(hb + 1) * 512], ps[:], AF.Gelu)))(),
                             reads=[pb], writes=[b_hf[t][hb]])
                    if 2 <= hb < 5:
                        emit_wld(hb + 3)

            def tail_amax(st):
                for hb in range(8):
                    for t in range(2):
                        P.op("vector",
                             (lambda t=t, hb=hb: (lambda h: h.tensor_reduce(
                                 qsh[:, t, 0, hb * 16:(hb + 1) * 16].rearrange("p (g o) -> p g o", o=1),
                                 hf[:, t, hb * 512:(hb + 1) * 512].rearrange("p (g e) -> p g e", e=32),
                                 AX.X, ALU.max, apply_absolute_value=True)))(),
                             reads=[b_hf[t][hb]], writes=[b_qsh[t]])

            def stage_tail(st):
                for t in range(2):
                    am = qsh[:, t, 0, :]
                    P.op("vector", (lambda am=am: (lambda h: h.tensor_scalar_max(am, am, TINY)))(),
                         reads=[b_qsh[t]], writes=[b_qsh[t]])
                    for th in scale_bits_thunks(b_qsh[t], qsh[:, t, :, :]):
                        th()
                for hb in range(8):
                    for t in range(2):
                        sc_b = qsh[:, t, 2, hb * 16:(hb + 1) * 16].to_broadcast([128, 16, 32])
                        rs_b = qsh[:, t, 3, hb * 16:(hb + 1) * 16].to_broadcast([128, 16, 32])
                        src3 = hf[:, t, hb * 512:(hb + 1) * 512].rearrange("p (g e) -> p g e", e=32)
                        dst3 = hqb[:, t, hb * 512:(hb + 1) * 512].rearrange("p (g e) -> p g e", e=32)
                        rsl = t
                        P.op("gpsimd", (lambda src3=src3, rs_b=rs_b, rsl=rsl: (lambda h: h.tensor_tensor(
                                zh[:, rsl, :].rearrange("p (g e) -> p g e", e=32), src3, rs_b, ALU.mult)))(),
                             reads=[b_hf[t][hb], b_qsh[t]], writes=[b_zh[rsl]])
                        P.op("vector", (lambda rsl=rsl: (lambda h: h.tensor_copy(mh[:, rsl, :], zh[:, rsl, :])))(),
                             reads=[b_zh[rsl]], writes=[b_mh[rsl]])
                        P.op("vector", (lambda dst3=dst3, sc_b=sc_b, t=t, rsl=rsl: (lambda h: h.tensor_tensor(
                                dst3, mh[:, rsl, :].rearrange("p (g e) -> p g e", e=32), sc_b, ALU.mult)))(),
                             reads=[b_mh[rsl], b_qsh[t]], writes=[b_hq[t][hb]])

            def _hqT_emit(st, t):
                for grp in range(4):
                        hps = (hT0, hT1)[grp % 2]
                        hpb = b_hT[grp % 2]
                        for j in range(8):
                            hc = grp * 8 + j
                            P.op("tensor",
                                 (lambda hps=hps, t=t, hc=hc, j=j: (lambda h: h.matmul(
                                     hps[:, j * 128:(j + 1) * 128], hqb[:, t, hc * 128:(hc + 1) * 128],
                                     ident_sb[:], start=(j == 0), stop=(j == 7), is_transpose=True)))(),
                                 reads=[b_hq[t][hc // 4], b_ident],
                                 wfirst=[hpb] if j == 0 else [], writes=[hpb], wlast=(j == 7))
                        P.op("scalar",
                             (lambda hps=hps, t=t, grp=grp: (lambda h: h.activation(
                                 hqT[:, grp * 8:(grp + 1) * 8, :],
                                 hps[:].rearrange("p (c q) -> p c q", q=128), AF.Identity)))(),
                             reads=[hpb], writes=[b_hqT[0]])

            def stage_proj_T(st):
                _hqT_emit(st, 0)

            def stage_proj_rest(st):
                for t in range(2):
                    ti = 2 * st + t
                    P.dma("sync", "x2i%d" % (2 + t), xC[:, t, :], x2_h[ti * 128:(ti + 1) * 128, :],
                          extra=[x2_bufs[ti].w], writes=[b_xC[t]])
                    if t == 1:
                        _hqT_emit(st, 1)
                    for half in range(2):
                        ps = (pj0, pj1)[half]
                        pb = b_pj[half]
                        for hc in range(32):
                            P.op("tensor",
                                 (lambda ps=ps, t=t, hc=hc, half=half: (lambda h: h.matmul(
                                     ps[:], hqT[:, hc, :], wpjT[:, hc, half * 512:(half + 1) * 512],
                                     start=(hc == 0), stop=False)))(),
                                 reads=[b_hqT[0], b_wpjT],
                                 wfirst=[pb] if hc == 0 else [], writes=[pb], wlast=False)
                        P.op("tensor",
                             (lambda ps=ps, half=half: (lambda h: h.matmul(
                                 ps[:], ones_sb[0:1, 0:128], bpj_sb[0:1, half * 512:(half + 1) * 512],
                                 start=False, stop=True)))(),
                             reads=[b_ones, b_bpj], writes=[pb])
                        P.op("vector",
                             (lambda ps=ps, t=t, half=half: (lambda h: h.tensor_tensor(
                                 xC[:, t, half * 512:(half + 1) * 512], ps[:],
                                 xC[:, t, half * 512:(half + 1) * 512], ALU.add)))(),
                             reads=[pb, b_xC[t]], writes=[b_xC[t]])
                    P.dma("scalar", "oo%d" % t, out_h[ti * 128:(ti + 1) * 128, :], xC[:, t, :],
                          reads=[b_xC[t]])

            for hb in range(2):
                P.dma("sync", "wre%d" % hb, wfcT_res[:, hb, :, :],
                      wfcT_h[:, :, hb * 512:(hb + 1) * 512].rearrange("f p q -> p f q"),
                      extra=[b.w for b in wfcT_bufs], writes=[b_wfcres[hb]])
            def front_a_all(st):
                fa_v1(st)
                fa_s(st)
                fa_v2(st)

            front_a_loads(0)
            front_a_all(0)
            front_a_T(0)
            if NST > 1:
                front_a_loads(1)
                front_a_all(1)
            for st in range(NST):
                if st + 2 < NST:
                    front_a_loads(st + 2)
                    fa_v1(st + 2)
                front_b(st)
                if st + 1 < NST:
                    front_a_T(st + 1)
                tail_amax(st)
                if st > 0:
                    stage_proj_T(st - 1)
                if st + 2 < NST:
                    fa_s(st + 2)
                    fa_v2(st + 2)
                if st > 0:
                    stage_proj_rest(st - 1)
                stage_tail(st)
            stage_proj_T(NST - 1)
            stage_proj_rest(NST - 1)

            P.items["sync"].append(("op", (lambda h: h.nop()),
                                    (("oo0", P.dma_cnt["oo0"]), ("oo1", P.dma_cnt["oo1"]))))
            P.cnt["sync"] += 1

            end = P.mark()
            _replay(nc, P, sems, cut, end, waited_all)

    return nc


_NC_CACHE = {}


def _get_nc(T):
    if T not in _NC_CACHE:
        _NC_CACHE[T] = build_program(T)
    return _NC_CACHE[T]


def kernel(L0_attn_x, L0_attn_Wc_down, L0_attn_Wq_up, L0_attn_Wkv_up, L0_attn_Wc_proj,
           L0_ffn_w_fc, L0_ffn_b_fc, L0_ffn_w_proj, L0_ffn_b_proj):
    x = np.ascontiguousarray(np.asarray(L0_attn_x, dtype=np.float32))
    B, T, _ = x.shape
    nc = _get_nc(T)
    ident = np.eye(128, dtype=np.float32)
    tri = np.triu(np.ones((128, 128), dtype=np.float32))
    sel = np.zeros((128, 256), dtype=np.float32)
    for m in range(4):
        sel[32 * m, m * 64:(m + 1) * 64] = 1.0
    common = {
        "wdn": np.ascontiguousarray(L0_attn_Wc_down, dtype=np.float32),
        "wqu": np.ascontiguousarray(L0_attn_Wq_up, dtype=np.float32),
        "wkv": np.ascontiguousarray(L0_attn_Wkv_up, dtype=np.float32),
        "wcp": np.ascontiguousarray(L0_attn_Wc_proj, dtype=np.float32),
        "wfc": np.ascontiguousarray(L0_ffn_w_fc, dtype=np.float32),
        "bfc": np.ascontiguousarray(np.asarray(L0_ffn_b_fc, dtype=np.float32).reshape(1, -1)),
        "wpj": np.ascontiguousarray(L0_ffn_w_proj, dtype=np.float32),
        "bpj": np.ascontiguousarray(np.asarray(L0_ffn_b_proj, dtype=np.float32).reshape(1, -1)),
        "ident": ident,
        "tri": tri,
        "sel": sel,
    }
    in_maps = [dict(common, x=np.ascontiguousarray(x[i])) for i in range(B)]
    res = run_bass_kernel_spmd(nc, in_maps, core_ids=list(range(B)))
    return np.stack([np.asarray(r["out"], dtype=np.float32) for r in res.results], axis=0)
